# version 19
# baseline (speedup 1.0000x reference)
import numpy as np
import concourse.bass as bass
import concourse.mybir as mybir
from concourse.bass_utils import run_bass_kernel_spmd

F32 = mybir.dt.float32
BF16 = mybir.dt.bfloat16
AF = mybir.ActivationFunctionType
ALU = mybir.AluOpType
AX = mybir.AxisListType

D = 1024
H = 16
DH = 64
S = 8192
B = 2
T = 2048
TT = T // 128
NB = S // 128
HG = 8
NG = 2
E = 8
NCH = D // 128
LN_EPS = 1e-5
RMS_EPS = 1e-8

ROLES = ("sync", "gpsimd", "tensor", "vector", "scalar")
DMA_SEMS = {"x", "xq", "w", "st", "sd", "pw"}

OFF_QNF = 0
OFF_O = OFF_QNF + 4 * 2048
OFF_KVW = OFF_O + 16 * 1040
OFF_XBLK = OFF_KVW + 8192
OFF_XN = OFF_XBLK + 2048
OFF_XNF = OFF_XN + 2048
OFF_KTOK = OFF_XNF + 2048
OFF_KNF = OFF_KTOK + 1024
OFF_VN = OFF_KNF + 2048
OFF_P = OFF_VN + 1040
OFF_QSQ = OFF_P + 2048
OFF_WSTG = OFF_QSQ + 512
ARENA_N = OFF_WSTG + 1024


class Walk:

    def __init__(self, nc, sems, role, eng):
        self.nc = nc
        self.sems = sems
        self.role = role
        self.eng = eng
        self.c = {k: 0 for k in sems}

    def op(self, role, f, inc=None):
        newv = None
        if inc is not None:
            if inc in DMA_SEMS and self.c[inc] > 0 and role == self.role:
                self.eng.wait_ge(self.sems[inc], self.c[inc])
            self.c[inc] += 16 if inc in DMA_SEMS else 1
            newv = self.c[inc]
        if role == self.role:
            ins = f(self.eng)
            if inc is not None:
                ins.then_inc(self.sems[inc], 16 if inc in DMA_SEMS else 1)
        return newv

    def wait(self, role, sem, val):
        if val is None or val <= 0:
            return
        assert val <= self.c[sem], f"wait {sem}>={val} but only {self.c[sem]} emitted"
        if role == self.role:
            self.eng.wait_ge(self.sems[sem], val)

    def drain(self, role):
        if role == self.role:
            self.eng.drain()


def build_program(dumps=(), nb=NB, tt_n=TT, stage=5, qt_mode=2):
    nc = bass.Bass("TRN2", target_bir_lowering=False)

    dx_q = nc.dram_tensor("xq", [T, D], F32, kind="ExternalInput").ap()
    dx_kv = nc.dram_tensor("xkv", [S, D], F32, kind="ExternalInput").ap()
    d_wq = nc.dram_tensor("wq", [D, D], F32, kind="ExternalInput").ap()
    d_wk = nc.dram_tensor("wk", [D, D], F32, kind="ExternalInput").ap()
    d_wv = nc.dram_tensor("wv", [D, D], F32, kind="ExternalInput").ap()
    d_wo = nc.dram_tensor("wo", [D, D], F32, kind="ExternalInput").ap()
    d_lng = nc.dram_tensor("lng", [D], F32, kind="ExternalInput").ap()
    d_lnb = nc.dram_tensor("lnb", [D], F32, kind="ExternalInput").ap()
    d_bo = nc.dram_tensor("bo", [D], F32, kind="ExternalInput").ap()
    d_qg8 = nc.dram_tensor("qg8", [8 * DH], F32, kind="ExternalInput").ap()
    d_kg8 = nc.dram_tensor("kg8", [8 * DH], F32, kind="ExternalInput").ap()
    d_gw = nc.dram_tensor("gw", [D, E], F32, kind="ExternalInput").ap()
    d_ew = nc.dram_tensor("ew", [E, D, D], F32, kind="ExternalInput").ap()
    d_y = nc.dram_tensor("y", [T, D], F32, kind="ExternalOutput").ap()
    d_ews = nc.dram_tensor("ewsplit", [E, 2, 128, NCH * 1024], BF16, kind="Internal").ap()
    d_wsp_hi = nc.dram_tensor("wsplit_hi", [128, 40, 512], BF16, kind="Internal").ap()
    d_wsp_lo = nc.dram_tensor("wsplit_lo", [128, 40, 512], BF16, kind="Internal").ap()

    idn = nc.inline_tensor(np.eye(128, dtype=np.float32), "idn")
    import ml_dtypes
    idnb = nc.inline_tensor(np.eye(128).astype(ml_dtypes.bfloat16), "idnb")

    AR = nc.alloc_sbuf_tensor("arena", [128, ARENA_N], F32).ap()
    sb_ident = nc.alloc_sbuf_tensor("ident", [128, 128], F32).ap()
    sb_identb = nc.alloc_sbuf_tensor("identb", [128, 128], BF16).ap()
    sb_lng = nc.alloc_sbuf_tensor("lngr", [128, D], F32).ap()
    sb_lnb = nc.alloc_sbuf_tensor("lnbr", [128, D], F32).ap()
    sb_bo = nc.alloc_sbuf_tensor("bor", [128, D], F32).ap()
    sb_qg = nc.alloc_sbuf_tensor("qgr", [128, 512], F32).ap()
    sb_kg = nc.alloc_sbuf_tensor("kgr", [128, 512], F32).ap()
    sb_gw = nc.alloc_sbuf_tensor("gwr", [128, 64], F32).ap()
    sb_gwb = nc.alloc_sbuf_tensor("gwb", [128, 128], BF16).ap()
    sb_mb = nc.alloc_sbuf_tensor("mbias", [128, 1], F32).ap()
    sb_c30 = nc.alloc_sbuf_tensor("c1em30", [128, 1], F32).ap()
    sb_junk = nc.alloc_sbuf_tensor("junk", [128, 4], F32).ap()
    sb_cln8 = nc.alloc_sbuf_tensor("cln8", [128, 1], F32).ap()
    sb_dumpS = nc.alloc_sbuf_tensor("dumpS", [128, D], BF16).ap()
    sb_dumpS2 = nc.alloc_sbuf_tensor("dumpS2", [128, D], BF16).ap()
    sb_masks = nc.alloc_sbuf_tensor("masks", [128, 64], F32).ap()
    sb_sc = nc.alloc_sbuf_tensor("scores", [128, 64], F32).ap()
    sb_mx = nc.alloc_sbuf_tensor("gmax", [128, 8], F32).ap()
    ST = {}
    for nm in ("sx", "sxx", "mu", "m2", "var", "sd", "rs", "mu1", "sd1", "rs1"):
        ST[nm] = nc.alloc_sbuf_tensor("st_" + nm, [128, 2], F32).ap()
    ST8 = {}
    for nm in ("ssq", "nrm", "rq"):
        ST8[nm] = nc.alloc_sbuf_tensor("st8_" + nm, [128, 16], F32).ap()

    pA = nc.alloc_psum_tensor("pA", [128, 1024], F32).ap()
    pB = nc.alloc_psum_tensor("pB", [128, 1024], F32).ap()
    pS0 = nc.alloc_psum_tensor("pS0", [128, 1024], F32).ap()
    pS1 = nc.alloc_psum_tensor("pS1", [128, 1024], F32).ap()
    pS = (pS0, pS1)
    pdst_map = {0: pA, 1: pB}

    sems = {k: nc.alloc_semaphore("sem_" + k) for k in ("x", "xq", "w", "t", "v", "s", "st", "sd", "g", "pw")}

    def ar(off, n):
        return AR[:, off:off + n]

    def qnf(pair):
        return ar(OFF_QNF + 2048 * pair, 2048)

    def o_tile(qt, g):
        return ar(OFF_O + 1040 * qt + 520 * g, 520)

    def x1_tile(tt):
        return ar(OFF_O + 1040 * tt, 1024)

    dump_specs = {
        "qnf": (ar(OFF_QNF, 8192), [128, 8192]),
        "O": (ar(OFF_O, 16640), [128, 16640]),
        "x1": (ar(OFF_O, 16640), [128, 16640]),
        "masks": (sb_masks[:, :], [128, 64]),
        "scores": (sb_sc[:, :], [128, 64]),
        "mbias": (sb_mb[:, :], [128, 1]),
        "knf": (ar(OFF_KNF, 2048), [128, 2048]),
        "vn": (ar(OFF_VN, 1040), [128, 1040]),
        "P": (ar(OFF_P, 2048), [128, 2048]),
    }
    dump_drams = {name: nc.dram_tensor("dump_" + name, dump_specs[name][1], F32,
                                       kind="ExternalOutput").ap()
                  for name in dumps}

    def schedule(w: Walk):
        def V(f, inc=None):
            return w.op("vector", f, inc)

        def Sc(f, inc=None):
            return w.op("scalar", f, inc)

        def Tn(f, inc=None):
            return w.op("tensor", f, inc)

        def Sy(f, inc=None):
            return w.op("sync", f, inc)

        def Gp(f, inc=None):
            return w.op("gpsimd", f, inc)

        guard_v = {}

        if dumps:
            V(lambda E: E.memset(AR[:, :], 0.0), inc="v")
            v_z = w.c["v"]
            w.wait("sync", "v", v_z)
            w.wait("gpsimd", "v", v_z)
            w.wait("scalar", "v", v_z)
            w.wait("tensor", "v", v_z)
        Sy(lambda E: E.dma_start(out=sb_lng[:, :], in_=d_lng[None, :].partition_broadcast(128)), inc="w")
        Sy(lambda E: E.dma_start(out=sb_lnb[:, :], in_=d_lnb[None, :].partition_broadcast(128)), inc="w")
        Sy(lambda E: E.dma_start(out=sb_bo[:, :], in_=d_bo[None, :].partition_broadcast(128)), inc="w")
        Sy(lambda E: E.dma_start(out=sb_qg[:, :], in_=d_qg8[None, :].partition_broadcast(128)), inc="w")
        Sy(lambda E: E.dma_start(out=sb_kg[:, :], in_=d_kg8[None, :].partition_broadcast(128)), inc="w")
        Sy(lambda E: E.dma_start(out=sb_ident[:, :], in_=idn.ap()[:, :]), inc="w")
        Sy(lambda E: E.dma_start(out=sb_identb[:, :], in_=idnb.ap()[:, :]), inc="w")
        for c in range(NCH):
            Sy(lambda E, c=c: E.dma_start(out=sb_gw[:, 8 * c:8 * c + 8],
                                          in_=d_gw[128 * c:128 * c + 128, :]), inc="w")
        w_params = w.c["w"]

        w.drain("vector")
        V(lambda E: E.memset(sb_c30[:, :], 1e-30))
        V(lambda E: E.memset(sb_cln8[:, :], 2.0794415416798357))
        V(lambda E: E.memset(ar(OFF_KNF, 2048), 0.0))
        w.wait("vector", "w", w_params)
        V(lambda E: E.tensor_reduce(ST["sx"][:, 0:1], sb_qg[:, 0:64], AX.X, ALU.max,
                                    apply_absolute_value=True))
        V(lambda E: E.tensor_reduce(ST["sxx"][:, 0:1], sb_kg[:, 0:64], AX.X, ALU.max,
                                    apply_absolute_value=True))
        w.drain("vector")
        V(lambda E: E.tensor_mul(sb_mb[:, :], ST["sx"][:, 0:1], ST["sxx"][:, 0:1]))
        w.drain("vector")
        V(lambda E: E.tensor_scalar(sb_mb[:, :], sb_mb[:, :], -8.0, None, ALU.mult))
        V(lambda E: E.tensor_scalar(sb_qg[:, :], sb_qg[:, :], 0.125, None, ALU.mult), inc="v")
        v_prologue = w.c["v"]
        w.wait("scalar", "v", v_prologue)

        def ln_chain_steps(par, x_ap, out_ap, mu_k, sd_k, rs_k, affine, s_dep=None,
                           v_pre=None):
            st = {}

            def s_sums():
                w.wait("scalar", "v", guard_v.get(("ln", par), 0))
                for sem, val in (s_dep or []):
                    w.wait("scalar", sem, val)
                Sc(lambda E: E.activation(sb_dumpS[:, :], x_ap, AF.Copy,
                                          accum_out=ST["sx"][:, par:par + 1]))
                Sc(lambda E: E.activation(sb_dumpS2[:, :], x_ap, AF.Square,
                                          accum_out=ST["sxx"][:, par:par + 1]), inc="s")
                st["sums"] = w.c["s"]
                w.wait("scalar", "s", st["sums"])
                Sc(lambda E: E.activation(ST[mu_k][:, par:par + 1],
                                          ST["sx"][:, par:par + 1], AF.Copy,
                                          scale=1.0 / D))
                Sc(lambda E: E.activation(ST[sd_k][:, par:par + 1],
                                          ST["sx"][:, par:par + 1], AF.Copy,
                                          scale=-1.0 / D))
                Sc(lambda E: E.activation(ST["m2"][:, par:par + 1],
                                          ST["sxx"][:, par:par + 1], AF.Copy,
                                          bias=LN_EPS, scale=1.0 / D), inc="s")
                st["stats"] = w.c["s"]

            def s_var():
                w.wait("vector", "s", st["stats"])
                V(lambda E: E.scalar_tensor_tensor(ST["var"][:, par:par + 1],
                                                   ST[mu_k][:, par:par + 1],
                                                   ST[sd_k][:, par:par + 1],
                                                   ST["m2"][:, par:par + 1],
                                                   ALU.mult, ALU.add), inc="v")
                st["var"] = w.c["v"]

            def s_rsqrt():
                w.wait("scalar", "v", st["var"])
                Sc(lambda E: E.activation(ST["sxx"][:, par:par + 1],
                                          ST["var"][:, par:par + 1], AF.Ln), inc="s")
                w.wait("scalar", "s", w.c["s"])
                Sc(lambda E: E.activation(ST[rs_k][:, par:par + 1],
                                          ST["sxx"][:, par:par + 1], AF.Exp,
                                          scale=-0.5), inc="s")
                st["rs"] = w.c["s"]

            def s_apply():
                w.wait("vector", "s", st["rs"])
                if v_pre:
                    for sem, val in v_pre:
                        w.wait("vector", sem, val)
                if affine:
                    V(lambda E: E.tensor_scalar(out_ap, x_ap, ST[mu_k][:, par:par + 1],
                                                ST[rs_k][:, par:par + 1],
                                                ALU.subtract, ALU.mult))
                    w.drain("vector")
                    V(lambda E: E.tensor_mul(out_ap, out_ap, sb_lng[:, :]))
                    w.drain("vector")
                    v_done = V(lambda E: E.tensor_add(out_ap, out_ap, sb_lnb[:, :]),
                               inc="v")
                else:
                    v_done = V(lambda E: E.tensor_scalar(out_ap, x_ap,
                                                         ST[mu_k][:, par:par + 1],
                                                         ST[rs_k][:, par:par + 1],
                                                         ALU.subtract, ALU.mult),
                               inc="v")
                guard_v[("ln", par)] = v_done
                st["done"] = v_done
                return v_done

            return [s_sums, s_var, s_rsqrt, s_apply], st

        def ln_chain(par, x_ap, out_ap, mu_k, sd_k, rs_k, affine, s_dep=None,
                     v_pre=None):
            steps, st = ln_chain_steps(par, x_ap, out_ap, mu_k, sd_k, rs_k, affine,
                                       s_dep, v_pre)
            for s in steps:
                s()
            return st["done"]

        def bfv(off_f32, n_bf16, start_bf16=0):
            n_f32 = (start_bf16 + n_bf16 + 1) // 2
            return ar(off_f32, n_f32).bitcast(BF16)[:, start_bf16:start_bf16 + n_bf16]

        def gp_split(dst_hi, dst_lo, src_f32):
            w.op("gpsimd", lambda E: E.tensor_copy(dst_hi, src_f32), inc="g")
            w.wait("gpsimd", "g", w.c["g"])
            return w.op("gpsimd", lambda E: E.tensor_sub(dst_lo, src_f32, dst_hi),
                        inc="g")

        def load_split(dram_chunk_fn, nchunk, width, dst_hi_fn, dst_lo_fn):
            g_done = None
            for c in range(nchunk):
                stg = ar(OFF_WSTG, width)
                w.wait("sync", "g", g_done if c >= 1 else w.c["g"])
                Sy(lambda E, c=c, stg=stg: E.dma_start(out=stg, in_=dram_chunk_fn(c)),
                   inc="w")
                w.wait("gpsimd", "w", w.c["w"])
                g_done = gp_split(dst_hi_fn(c), dst_lo_fn(c), stg)
            return g_done

        def load_split_vec(dram_chunk_fn, nchunk, width, dst_hi_fn, dst_lo_fn):
            vd = {}
            for c in range(nchunk):
                stg = ar(OFF_WSTG + 512 * (c % 2), width)
                w.wait("sync", "v", vd.get(c - 2, w.c["v"] if c < 2 else 0))
                Sy(lambda E, c=c, stg=stg: E.dma_start(out=stg, in_=dram_chunk_fn(c)),
                   inc="w")
                w.wait("vector", "w", w.c["w"])
                V(lambda E, c=c, stg=stg: E.tensor_copy(dst_hi_fn(c), stg), inc="v")
                w.wait("vector", "v", w.c["v"])
                vd[c] = V(lambda E, c=c, stg=stg: E.tensor_sub(dst_lo_fn(c), stg,
                                                               dst_hi_fn(c)), inc="v")
            return vd[nchunk - 1]

        def load_split_gpself(dram_chunk_fn, nchunk, dst_hi_fn, dst_lo_fn, v_pre):
            w.wait("gpsimd", "v", v_pre)
            for c in range(nchunk):
                stg = ar(OFF_WSTG, 512)
                w.wait("gpsimd", "g", w.c["g"])
                Gp(lambda E, c=c, stg=stg: E.dma_start(out=stg, in_=dram_chunk_fn(c)),
                   inc="pw")
                w.wait("gpsimd", "pw", w.c["pw"])
                gp_split(dst_hi_fn(c), dst_lo_fn(c), stg)
            return w.c["g"]

        def mm3(psum_ap, lhs_hi_fn, lhs_lo_fn, rhs_hi_fn, rhs_lo_fn, nchunk,
                inc_last=True, lo_wait=None, pass1_wait=None):
            last = None
            n = 0
            total = nchunk * 3
            for p, (lf, rf) in enumerate(((lhs_hi_fn, rhs_hi_fn),
                                          (lhs_hi_fn, rhs_lo_fn),
                                          (lhs_lo_fn, rhs_hi_fn))):
                if p == 2 and lo_wait is not None:
                    w.wait("tensor", lo_wait[0], lo_wait[1])
                for c in range(nchunk):
                    if p == 0 and pass1_wait is not None and c == pass1_wait[0]:
                        w.wait("tensor", pass1_wait[1], pass1_wait[2])
                    n += 1
                    last = Tn(lambda E, c=c, lf=lf, rf=rf, first=(n == 1),
                              stop=(n == total): E.matmul(
                                  psum_ap, lf(c), rf(c), start=first, stop=stop),
                              inc="t" if (inc_last and n == total) else None)
            return last

        def transpose8(src_fn, dst_ap, nchunk, t_deps=(), v_extra=(), split_to=None,
                       staged=False):
            for sem, val in t_deps:
                w.wait("tensor", sem, val)
            if staged:
                t_h = None
                for c in range(nchunk):
                    t_h = Tn(lambda E, c=c: E.transpose(pB[:, 128 * c:128 * c + 128],
                                                        src_fn(c), sb_ident[:, :]),
                             inc="t" if c in (3, nchunk - 1) else None)
                    if c == 3:
                        t_half = t_h
                t_tr = t_h
                hi_ap, lo_ap = split_to
                w.wait("vector", "t", t_half)
                for sem, val in v_extra:
                    w.wait("vector", sem, val)
                v_hi1 = V(lambda E: E.tensor_copy(hi_ap[:, 0:512], pB[:, 0:512]),
                          inc="v")
                w.wait("vector", "t", t_tr)
                v_hi2 = V(lambda E: E.tensor_copy(hi_ap[:, 512:1024], pB[:, 512:1024]),
                          inc="v")
                w.wait("vector", "v", v_hi2)
                done = V(lambda E: E.tensor_sub(lo_ap[:, 0:1024], pB[:, 0:1024],
                                                hi_ap[:, 0:1024]), inc="v")
                transpose8.v_hi = v_hi1
                transpose8.v_hi2 = v_hi2
                return done, t_tr
            t_tr = None
            for c in range(nchunk):
                t_tr = Tn(lambda E, c=c: E.transpose(pB[:, 128 * c:128 * c + 128],
                                                     src_fn(c), sb_ident[:, :]),
                          inc="t" if c == nchunk - 1 else None)
            w.wait("vector", "t", t_tr)
            for sem, val in v_extra:
                w.wait("vector", sem, val)
            done = None
            if split_to is None:
                for h in range((nchunk + 3) // 4):
                    n = min(4, nchunk - 4 * h) * 128
                    done = V(lambda E, h=h, n=n: E.tensor_copy(
                        dst_ap[:, 512 * h:512 * h + n], pB[:, 512 * h:512 * h + n]),
                        inc="v")
                transpose8.v_hi = done
            else:
                hi_ap, lo_ap = split_to
                nb_ = nchunk * 128
                v_hi = V(lambda E: E.tensor_copy(hi_ap[:, 0:nb_], pB[:, 0:nb_]),
                         inc="v")
                w.wait("vector", "v", v_hi)
                done = V(lambda E: E.tensor_sub(lo_ap[:, 0:nb_], pB[:, 0:nb_],
                                                hi_ap[:, 0:nb_]), inc="v")
                transpose8.v_hi = v_hi
            return done, t_tr

        def rms_chain(par, psum_ap, out_ap, g_rep, t_dep):
            w.wait("scalar", "t", t_dep)
            w.wait("scalar", "v", guard_v.get(("qsq",), 0))
            Sc(lambda E: E.activation(ar(OFF_QSQ, 512), psum_ap, AF.Square), inc="s")
            s_sq = w.c["s"]
            w.wait("vector", "s", s_sq)
            V(lambda E: E.tensor_reduce(ST8["ssq"][:, 8 * par:8 * par + 8],
                                        ar(OFF_QSQ, 512).rearrange("p (h c) -> p h c", h=8),
                                        AX.X, ALU.add), inc="v")
            guard_v[("qsq",)] = w.c["v"]
            v_ssq = w.c["v"]
            w.wait("scalar", "v", max(v_ssq, guard_v.get(("rms", par), 0)))
            Sc(lambda E: E.activation(ST8["nrm"][:, 8 * par:8 * par + 8],
                                      ST8["ssq"][:, 8 * par:8 * par + 8],
                                      AF.Ln, bias=sb_c30[:, 0:1]), inc="s")
            w.wait("scalar", "s", w.c["s"])
            Sc(lambda E: E.activation(ST8["rq"][:, 8 * par:8 * par + 8],
                                      ST8["nrm"][:, 8 * par:8 * par + 8],
                                      AF.Exp, scale=-0.5, bias=sb_cln8[:, 0:1]), inc="s")
            s_rq = w.c["s"]
            w.wait("vector", "s", s_rq)
            V(lambda E: E.tensor_scalar(ST8["rq"][:, 8 * par:8 * par + 8],
                                        ST8["rq"][:, 8 * par:8 * par + 8],
                                        1e8, None, ALU.min))
            w.drain("vector")
            V(lambda E: E.tensor_mul(
                out_ap.rearrange("p (h c) -> p h c", h=8),
                psum_ap.rearrange("p (h c) -> p h c", h=8),
                ST8["rq"][:, 8 * par:8 * par + 8].unsqueeze(2).to_broadcast([128, 8, 64])))
            w.drain("vector")
            v_done = V(lambda E: E.tensor_mul(out_ap, out_ap, g_rep), inc="v")
            guard_v[("rms", par)] = v_done
            return v_done

        evac_done = {}
        exp_done = {}
        pv_done = {}
        last_exp = {}
        last_pv = {}
        last_evac = {}
        blk_last_pv = {}
        sim_last = {}
        t_kv_done = {}
        t_kntr = {}
        t_xntr = {}
        xn_ready = {}
        qproj_t = {}
        x_cnt = {}

        w.wait("gpsimd", "w", w_params)
        gp_split(sb_gwb[:, 0:64], sb_gwb[:, 64:128], sb_gw[:, :])
        g_gw = w.c["g"]

        presplit_done = [0]
        pw_ready = {}
        presplit_work = []
        for widx, dw in ((0, d_wq), (1, d_wk), (2, d_wv)):
            for c in range(NCH):
                presplit_work.append((
                    lambda c=c, dw=dw: dw[128 * c:128 * c + 128, 512:1024],
                    d_wsp_hi[:, 8 * widx + c, :], d_wsp_lo[:, 8 * widx + c, :],
                    ("wq1", "wk1", "wv1")[widx] if c == NCH - 1 else None))
        for c in range(NCH):
            for half in range(2):
                p = 24 + 2 * c + half
                presplit_work.append((
                    lambda c=c, half=half: d_wo[128 * c:128 * c + 128,
                                                512 * half:512 * half + 512],
                    d_wsp_hi[:, p, :], d_wsp_lo[:, p, :],
                    "wo" if p == 39 else None))
        for e in range(E):
            for c in range(NCH):
                for h in range(2):
                    presplit_work.append((
                        lambda e=e, c=c, h=h: d_ew[e, 128 * c:128 * c + 128,
                                                   512 * h:512 * h + 512],
                        d_ews[e, h, :, 1024 * c:1024 * c + 512],
                        d_ews[e, h, :, 1024 * c + 512:1024 * c + 1024],
                        "ew" if (e, c, h) == (E - 1, NCH - 1, 1) else None))

        def presplit_step():
            k = presplit_done[0]
            if k >= len(presplit_work):
                return
            presplit_done[0] += 1
            src_fn, dst_hi, dst_lo, tag = presplit_work[k]
            stg = ar(OFF_WSTG, 512)
            oh = bfv(OFF_WSTG + 512, 512)
            olo = bfv(OFF_WSTG + 512, 512, 512)
            w.wait("gpsimd", "g", w.c["g"])
            w.wait("gpsimd", "pw", w.c["pw"])
            w.wait("gpsimd", "w", w.c["w"])
            Gp(lambda E_, src_fn=src_fn, stg=stg: E_.dma_start(
                out=stg, in_=src_fn()), inc="pw")
            w.wait("gpsimd", "pw", w.c["pw"])
            gp_split(oh, olo, stg)
            Gp(lambda E_, dst_hi=dst_hi, oh=oh: E_.dma_start(
                out=dst_hi, in_=oh), inc="pw")
            Gp(lambda E_, dst_lo=dst_lo, olo=olo: E_.dma_start(
                out=dst_lo, in_=olo), inc="pw")
            if tag is not None:
                pw_ready[tag] = w.c["pw"]

        def presplit_flush(tag):
            while tag not in pw_ready:
                presplit_step()
            w.wait("sync", "pw", pw_ready[tag])

        for g in (range(NG) if stage >= 2 else ()):
            if g > 0:
                w.wait("sync", "t", w.c["t"])
                w.wait("tensor", "v", w.c["v"])
            w.wait("sync", "g", w.c["g"])

            def q_dma(tt):
                par = tt % 2
                if tt >= 2:
                    w.wait("sync", "t", qproj_t[(g, tt - 2)])
                elif g > 0:
                    w.wait("sync", "v", w.c["v"])
                Sy(lambda E, tt=tt, par=par: E.dma_start(
                    out=ar(OFF_XBLK + 1024 * par, 1024),
                    in_=dx_q[128 * tt:128 * tt + 128, :]), inc="xq")
                x_cnt[("q", g, tt)] = w.c["xq"]

            if g == 0:
                q_dma(0)
                if tt_n > 1:
                    q_dma(1)
            if g == 0:
                load_split_vec(lambda c: d_wq[128 * c:128 * c + 128, 0:512],
                               NCH, 512,
                               lambda c: bfv(OFF_KVW, 512, 512 * c),
                               lambda c: bfv(OFF_KVW, 512, 4096 + 512 * c))
                wq_dep = ("v", w.c["v"])
                wk_stage_v = load_split_vec(
                    lambda c: d_wk[128 * c:128 * c + 128, 0:512],
                    NCH, 512,
                    lambda c: bfv(OFF_O, 512, 512 * c),
                    lambda c: bfv(OFF_O, 512, 4096 + 512 * c))
                g_wv0 = load_split_gpself(
                    lambda c: d_wv[128 * c:128 * c + 128, 0:512],
                    NCH,
                    lambda c: bfv(OFF_KVW, 512, 8192 + 512 * c),
                    lambda c: bfv(OFF_KVW, 512, 12288 + 512 * c), wk_stage_v)
            else:
                presplit_flush("wq1")
                Sy(lambda E: E.dma_start(
                    out=ar(OFF_KVW, 2048).bitcast(BF16).rearrange("p (c n) -> p c n", c=8),
                    in_=d_wsp_hi[:, 0:8, :]), inc="w")
                Sy(lambda E: E.dma_start(
                    out=ar(OFF_KVW + 2048, 2048).bitcast(BF16).rearrange("p (c n) -> p c n", c=8),
                    in_=d_wsp_lo[:, 0:8, :]), inc="w")
                wq_dep = ("w", w.c["w"])
                presplit_flush("wv1")
                Sy(lambda E: E.dma_start(
                    out=ar(OFF_KVW + 4096, 2048).bitcast(BF16).rearrange("p (c n) -> p c n", c=8),
                    in_=d_wsp_hi[:, 16:24, :]), inc="w")
                Sy(lambda E: E.dma_start(
                    out=ar(OFF_KVW + 6144, 2048).bitcast(BF16).rearrange("p (c n) -> p c n", c=8),
                    in_=d_wsp_lo[:, 16:24, :]), inc="w")

            def q_ln_steps(tt):
                par = tt % 2
                xb = ar(OFF_XBLK + 1024 * par, 1024)
                xn = ar(OFF_XN + 1024 * par, 1024)
                v_pre = [("t", qproj_t[(g, tt - 2)])] if tt >= 2 else None
                steps0, st0 = ln_chain_steps(par, xb, xn, "mu", "sd", "rs", False,
                                             s_dep=[("xq", x_cnt[("q", g, tt)])],
                                             v_pre=v_pre)

                def s5():
                    steps1, st1 = ln_chain_steps(par, xn, xn, "mu1", "sd1", "rs1",
                                                 True, s_dep=[("v", st0["done"])])
                    st0["steps1"] = steps1
                    st0["st1"] = st1
                    steps1[0]()

                def mk(i2):
                    def f():
                        st0["steps1"][i2]()
                        if i2 == 3:
                            q_xn_ready[tt] = st0["st1"]["done"]
                    return f

                return steps0 + [s5, mk(1), mk(2), mk(3)]

            q_xn_ready = {}
            q_evac = {}
            q_pending = []
            if g > 0:
                q_dma(0)
            for s_ in q_ln_steps(0):
                s_()
            if tt_n > 1:
                if g > 0:
                    q_dma(1)
                q_pending[:] = q_ln_steps(1)
            qn_tr_cnt = {}
            v_xnf_prev = last_evac.get(1, 0)
            defer_qn = None

            def make_qn(tt, par, v_stg):
                def f():
                    w.wait("tensor", "v", v_stg)
                    w.wait("tensor", "v", q_evac.get(tt - 2, 0))
                    pS1b = pS1.bitcast(BF16)
                    for h2 in range(8):
                        t_qtr = Tn(lambda E, h2=h2, par=par: E.transpose(
                            pS1b[:, 1024 * par + 128 * h2:1024 * par + 128 * h2 + 128],
                            ar(OFF_P + 512 * par, 512).bitcast(BF16)[:, 128 * h2:128 * h2 + 128],
                            sb_identb[:, :]), inc="t" if h2 == 7 else None)
                    qn_tr_cnt[tt] = w.c["t"]
                    w.wait("vector", "t", qn_tr_cnt[tt])
                    for h2 in range(8):
                        V(lambda E, h2=h2, tt=tt, par=par: E.tensor_copy(
                            bfv(OFF_QNF, 128, 2048 * h2 + 128 * tt),
                            pS1.bitcast(BF16)[:, 1024 * par + 128 * h2:1024 * par + 128 * h2 + 128]),
                          inc="v" if h2 == 7 else None)
                    q_evac[tt] = w.c["v"]
                return f

            for tt in range(tt_n):
                par = tt % 2
                xnf = ar(OFF_XNF + 1024 * par, 1024)
                v_xn = q_xn_ready[tt]
                tdeps = [("v", v_xn), ("v", v_xnf_prev)]
                v_xnf, _ = transpose8(lambda c, par=par: ar(OFF_XN + 1024 * par + 128 * c, 128),
                                      xnf, 8, t_deps=tdeps,
                                      split_to=(bfv(OFF_XNF + 1024 * par, 1024),
                                                bfv(OFF_XNF + 1024 * par, 1024, 1024)),
                                      staged=True)
                v_xnf_prev = v_xnf
                v_hi1, v_hi2 = transpose8.v_hi, transpose8.v_hi2
                for _ in range(3):
                    if q_pending:
                        q_pending.pop(0)()
                w.wait("tensor", "v", v_hi1)
                w.wait("tensor", "v", guard_v.get(("rms", par), 0))
                w.wait("tensor", "s", last_exp.get(par, 0))
                w.wait("tensor", *wq_dep)
                qproj_t[(g, tt)] = mm3(
                    pS0[:, 512 * par:512 * par + 512],
                    lambda c, par=par: bfv(OFF_XNF + 1024 * par, 128, 128 * c),
                    lambda c, par=par: bfv(OFF_XNF + 1024 * par, 128, 1024 + 128 * c),
                    lambda c: bfv(OFF_KVW, 512, 512 * c),
                    lambda c: bfv(OFF_KVW, 512, 4096 + 512 * c), NCH,
                    lo_wait=("v", v_xnf), pass1_wait=(4, "v", v_hi2))
                if defer_qn is not None:
                    defer_qn()
                if tt + 2 < tt_n:
                    q_dma(tt + 2)
                while q_pending:
                    q_pending.pop(0)()
                w.wait("vector", "t", qproj_t[(g, tt)])
                qn_tok = ar(OFF_KTOK + 512 * par, 512)
                rms_chain(par, pS0[:, 512 * par:512 * par + 512], qn_tok,
                          sb_qg[:, :], w.c["t"])
                stg_b = ar(OFF_P + 512 * par, 512).bitcast(BF16).rearrange(
                    "p (h c) -> p h c", h=8)
                qt3 = qn_tok.rearrange("p (h c) -> p h c", h=8)
                w.wait("vector", "v", w.c["v"])
                w.wait("vector", "t", qn_tr_cnt.get(tt - 2, 0))
                V(lambda E, stg_b=stg_b, qt3=qt3: E.tensor_copy(stg_b[:, :, 0:64], qt3),
                  inc="v")
                w.wait("vector", "v", w.c["v"])
                v_stg = V(lambda E, stg_b=stg_b, qt3=qt3: E.tensor_sub(
                    stg_b[:, :, 64:128], qt3, stg_b[:, :, 0:64]), inc="v")
                defer_qn = make_qn(tt, par, v_stg)
                if tt + 2 < tt_n:
                    q_pending[:] = q_ln_steps(tt + 2)
            defer_qn()
            v_qnf = w.c["v"]

            w.wait("sync", "t", w.c["t"])
            w.wait("sync", "g", w.c["g"])
            if g == 0:
                w.wait("vector", "t", w.c["t"])
                V(lambda E: E.tensor_copy(bfv(OFF_KVW, 4096),
                                          bfv(OFF_O, 4096)), inc="v")
                V(lambda E: E.tensor_copy(bfv(OFF_KVW, 4096, 4096),
                                          bfv(OFF_O, 4096, 4096)), inc="v")
                g_kv_dep = [("v", w.c["v"]), ("g", g_wv0)]
            else:
                presplit_flush("wk1")
                Sy(lambda E: E.dma_start(
                    out=ar(OFF_KVW, 2048).bitcast(BF16).rearrange("p (c n) -> p c n", c=8),
                    in_=d_wsp_hi[:, 8:16, :]), inc="w")
                Sy(lambda E: E.dma_start(
                    out=ar(OFF_KVW + 2048, 2048).bitcast(BF16).rearrange("p (c n) -> p c n", c=8),
                    in_=d_wsp_lo[:, 8:16, :]), inc="w")
                g_kv_dep = [("w", w.c["w"])]

            def emit_xdma(jj):
                par2 = jj % 2
                if jj >= 2:
                    w.wait("gpsimd", "v", xn_ready[(g, jj - 2)])
                else:
                    w.wait("gpsimd", "v",
                           q_xn_ready.get(tt_n - 2 + par2, w.c["v"]))
                Gp(lambda E, jj=jj, par2=par2: E.dma_start(
                    out=ar(OFF_XBLK + 1024 * par2, 1024),
                    in_=dx_kv[128 * jj:128 * jj + 128, :]), inc="x")
                x_cnt[("k", g, jj)] = w.c["x"]

            def ln_step_list(jj):
                par2 = jj % 2
                xb2 = ar(OFF_XBLK + 1024 * par2, 1024)
                xn2 = ar(OFF_XN + 1024 * par2, 1024)
                v_pre2 = ([("t", t_xntr[(g, jj - 2)])] if jj >= 2
                          else [("t", w.c["t"])])
                steps0, st0 = ln_chain_steps(par2, xb2, xn2, "mu", "sd", "rs", False,
                                             s_dep=[("x", x_cnt[("k", g, jj)])],
                                             v_pre=v_pre2)
                def s5():
                    steps1, st1 = ln_chain_steps(par2, xn2, xn2, "mu1", "sd1", "rs1",
                                                 True, s_dep=[("v", st0["done"])])
                    st0["steps1"] = steps1
                    st0["st1"] = st1
                    steps1[0]()

                def mk(i):
                    def f():
                        st0["steps1"][i]()
                        if i == 3:
                            xn_ready[(g, jj)] = st0["st1"]["done"]
                    return f

                return steps0 + [s5, mk(1), mk(2), mk(3)]

            def emit_ln(jj):
                for s in ln_step_list(jj):
                    s()

            ln_slot = 6 if tt_n >= 14 else None
            pending_ln = []
            w.wait("vector", "t", w.c["t"])
            for par2 in range(2):
                V(lambda E, par2=par2: E.memset(
                    ar(OFF_VN + 520 * par2, 520).rearrange(
                        "p (h c) -> p h c", h=8)[:, :, 64:65], 1.0), inc="v")
            pending_ln = []
            emit_xdma(0)
            emit_ln(0)
            if nb > 1:
                emit_xdma(1)
                if ln_slot is None:
                    pass
            for j in (range(nb) if stage >= 3 else ()):
                par = j % 2
                xnf = ar(OFF_XNF + 1024 * par, 1024)
                ktok = ar(OFF_KTOK + 512 * par, 512)
                knf_base = OFF_KNF + 1024 * par
                vn = ar(OFF_VN + 520 * par, 520)
                if ln_slot is None and j >= 1:
                    emit_ln(j)
                tdeps = [("v", xn_ready[(g, j)]), ("v", last_evac.get(1))]
                v_extra = [("t", t_kv_done[(g, j - 2)])] if j >= 2 else ()
                v_xnf, t_xntr[(g, j)] = transpose8(
                    lambda c, par=par: ar(OFF_XN + 1024 * par + 128 * c, 128),
                    xnf, 8, t_deps=tdeps, v_extra=v_extra,
                    split_to=(bfv(OFF_XNF + 1024 * par, 1024),
                              bfv(OFF_XNF + 1024 * par, 1024, 1024)), staged=True)
                v_xnf_hi = transpose8.v_hi
                w.wait("tensor", "v", v_xnf_hi)
                for dep_ in g_kv_dep:
                    w.wait("tensor", *dep_)
                w.wait("tensor", "v", last_evac.get(0))
                xh = lambda c, par=par: bfv(OFF_XNF + 1024 * par, 128, 128 * c)
                xl = lambda c, par=par: bfv(OFF_XNF + 1024 * par, 128, 1024 + 128 * c)
                t_k_done = mm3(pA[:, 0:512], xh, xl,
                               lambda c: bfv(OFF_KVW, 512, 512 * c),
                               lambda c: bfv(OFF_KVW, 512, 4096 + 512 * c), NCH,
                               lo_wait=("v", v_xnf),
                               pass1_wait=(4, "v", transpose8.v_hi2))
                t_kv_done[(g, j)] = mm3(pA[:, 512:1024], xh, xl,
                                        lambda c: bfv(OFF_KVW, 512, 8192 + 512 * c),
                                        lambda c: bfv(OFF_KVW, 512, 12288 + 512 * c), NCH)
                w.wait("vector", "t", t_k_done)
                if j >= 1:
                    w.wait("vector", "t", t_kntr[(g, j - 1)])
                rms_chain(par, pA[:, 0:512], ktok, sb_kg[:, :], t_k_done)
                stgA = ar(OFF_QSQ, 512).bitcast(BF16).rearrange("p (h c) -> p h c", h=8)
                stgB = ar(OFF_KTOK + 512 * (1 - par), 512).bitcast(BF16).rearrange(
                    "p (h c) -> p h c", h=8)
                kt3 = ktok.rearrange("p (h c) -> p h c", h=8)
                w.wait("vector", "v", w.c["v"])
                if j >= 1:
                    w.wait("vector", "t", t_kntr[(g, j - 1)])
                V(lambda E, stgA=stgA, kt3=kt3: E.tensor_copy(stgA[:, :, 0:64], kt3),
                  inc="v")
                w.wait("vector", "v", w.c["v"])
                v_stgA = V(lambda E, stgA=stgA, kt3=kt3: E.tensor_sub(
                    stgA[:, :, 64:128], kt3, stgA[:, :, 0:64]), inc="v")
                w.wait("vector", "v", v_stgA)
                srcswap = ar(OFF_QSQ, 512).bitcast(BF16).rearrange(
                    "p (h s c) -> p h s c", h=8, s=2)[:, :, ::-1, :]
                v_stgk = V(lambda E, stgB=stgB, srcswap=srcswap: E.tensor_copy(
                    stgB.rearrange("p h (s c) -> p h s c", s=2), srcswap), inc="v")
                guard_v[("qsq",)] = v_stgk
                w.wait("vector", "t", t_kv_done[(g, j)])
                if j >= 2:
                    w.wait("vector", "t", blk_last_pv.get((g, j - 2)))
                v_vn = V(lambda E, par=par: E.tensor_copy(
                    ar(OFF_VN + 520 * par, 520).rearrange("p (h c) -> p h c", h=8)[:, :, 0:64],
                    pA[:, 512:1024].rearrange("p (h c) -> p h c", h=8)), inc="v")
                v_ktok = w.c["v"]
                w.wait("tensor", "v", v_stgA)
                if j >= 2 and qt_mode >= 1:
                    w.wait("tensor", "t", sim_last.get((g, j - 2)))
                pBb = pB.bitcast(BF16)
                for h in range(HG):
                    t_kntrA = Tn(lambda E, h=h, par=par: E.transpose(
                        pBb[:, 128 * h:128 * h + 128],
                        ar(OFF_QSQ, 512).bitcast(BF16)[:, 128 * h:128 * h + 128],
                        sb_identb[:, :]), inc="t" if h == HG - 1 else None)
                w.wait("vector", "t", t_kntrA)
                v_knfA = V(lambda E, knf_base=knf_base: E.tensor_copy(
                    bfv(knf_base, 1024, 0), pB.bitcast(BF16)[:, 0:1024]), inc="v")
                w.wait("tensor", "v", v_stgk)
                for h in range(HG):
                    t_kntr[(g, j)] = Tn(lambda E, h=h, par=par: E.transpose(
                        pBb[:, 1024 + 128 * h:1024 + 128 * h + 128],
                        ar(OFF_KTOK + 512 * (1 - par), 512).bitcast(BF16)[:, 128 * h:128 * h + 128],
                        sb_identb[:, :]), inc="t" if h == HG - 1 else None)
                w.wait("vector", "t", t_kntr[(g, j)])
                v_knfB = V(lambda E, knf_base=knf_base: E.tensor_copy(
                    bfv(knf_base, 1024, 1024),
                    pB.bitcast(BF16)[:, 1024:2048]), inc="v")

                def emit_pv(qt):
                    qpar = qt % 2
                    w.wait("tensor", "s", exp_done[(g, j, qt)])
                    w.wait("tensor", "v", v_vn)
                    w.wait("tensor", "v", last_evac.get(qpar))
                    for h in range(HG):
                        col = 65 * (h % 4) + 512 * (h // 4)
                        r2 = Tn(lambda E, h=h, col=col, qpar=qpar, par=par: E.matmul(
                            pdst_map[qpar][:, col:col + 65],
                            ar(OFF_P + 1024 * qpar + 128 * h, 128),
                            ar(OFF_VN + 520 * par, 520)[:, 65 * h:65 * h + 65],
                            start=True, stop=True),
                            inc="t" if h == HG - 1 else None)
                    pv_done[(g, j, qt)] = r2
                    last_pv[qpar] = r2
                    w.wait("vector", "t", pv_done[(g, j, qt)])
                    if j == 0:
                        V(lambda E, qt=qt, g=g, qpar=qpar: E.tensor_copy(
                            o_tile(qt, g)[:, 0:260], pdst_map[qpar][:, 0:260]))
                        evac_done[(g, j, qt)] = V(lambda E, qt=qt, g=g, qpar=qpar: E.tensor_copy(
                            o_tile(qt, g)[:, 260:520], pdst_map[qpar][:, 512:772]), inc="v")
                    else:
                        V(lambda E, qt=qt, g=g, qpar=qpar: E.tensor_add(
                            o_tile(qt, g)[:, 0:260], o_tile(qt, g)[:, 0:260],
                            pdst_map[qpar][:, 0:260]))
                        evac_done[(g, j, qt)] = V(lambda E, qt=qt, g=g, qpar=qpar: E.tensor_add(
                            o_tile(qt, g)[:, 260:520], o_tile(qt, g)[:, 260:520],
                            pdst_map[qpar][:, 512:772]), inc="v")
                    last_evac[qpar] = evac_done[(g, j, qt)]

                qt_seq = list(range(tt_n))
                if tt_n >= 4:
                    qt_seq = list(range(tt_n - 2)) + [tt_n - 1, tt_n - 2]
                for pos, qt in (enumerate(qt_seq) if qt_mode >= 1 else ()):
                    qpar = qt % 2
                    w.wait("tensor", "v", v_knfB)
                    if qt == 0 and j == 0:
                        w.wait("tensor", "v", v_qnf)
                    w.wait("tensor", "s", last_exp.get(qpar))
                    for h in range(HG):
                        Tn(lambda E, h=h, qt=qt, par=par, qpar=qpar:
                           E.matmul(pS[qpar][:, 128 * h:128 * h + 128],
                                    bfv(OFF_KNF + 1024 * par, 128, 128 * h),
                                    bfv(OFF_QNF, 128, 2048 * h + 128 * qt),
                                    start=True, stop=False))
                        t_sim = Tn(lambda E, h=h, qt=qt, par=par, qpar=qpar:
                                   E.matmul(pS[qpar][:, 128 * h:128 * h + 128],
                                            bfv(OFF_KNF + 1024 * par, 128, 1024 + 128 * h),
                                            bfv(OFF_QNF, 128, 2048 * h + 128 * qt),
                                            start=False, stop=True),
                                   inc="t" if h == HG - 1 else None)
                    if pos == tt_n - 1:
                        sim_last[(g, j)] = t_sim
                    if qt_mode >= 2 and pos >= 1:
                        emit_pv(qt_seq[pos - 1])
                    w.wait("scalar", "t", t_sim)
                    if qt_mode >= 2:
                        w.wait("scalar", "t", last_pv.get(qpar))
                    exp_done[(g, j, qt)] = Sc(lambda E, qpar=qpar: E.activation(
                        ar(OFF_P + 1024 * qpar, 1024), pS[qpar][:, :], AF.Exp,
                        bias=sb_mb[:, 0:1]), inc="s")
                    last_exp[qpar] = exp_done[(g, j, qt)]
                    if qt_mode < 2:
                        pv_done[(g, j, qt)] = None
                        evac_done[(g, j, qt)] = None
                    if pos == 0 and j + 1 < nb and j >= 1:
                        emit_xdma(j + 1)
                    if pos in (2, 9):
                        presplit_step()
                    if ln_slot is not None and j + 1 < nb and 5 <= pos < 13:
                        if pos == 5:
                            pending_ln[:] = ln_step_list(j + 1)
                        pending_ln.pop(0)()
                if qt_mode >= 2 and tt_n >= 1:
                    emit_pv(qt_seq[-1])
                blk_last_pv[(g, j)] = w.c["t"]
                if ln_slot is None and j + 1 < nb and j >= 1:
                    emit_xdma(j + 1)

        if stage < 4:
            joins = dict(w.c)
            for sem in ("t", "v", "s"):
                w.wait("gpsimd", sem, joins[sem])
            for name in dumps:
                Gp(lambda E, name=name: E.dma_start(
                    out=dump_drams[name][:, :], in_=dump_specs[name][0]), inc="st")
            w.wait("gpsimd", "st", w.c["st"])
            w.wait("sync", "st", w.c["st"])
            return
        joins = dict(w.c)
        for r in ROLES:
            for sem in ("t", "v", "s"):
                w.wait(r, sem, joins[sem])
        w.wait("sync", "g", w.c["g"])
        presplit_flush("wo")
        Sy(lambda E: E.dma_start(
            out=ar(OFF_KVW, 4096).bitcast(BF16).rearrange("p (k n) -> p k n", k=16),
            in_=d_wsp_hi[:, 24:40, :]), inc="w")
        wo_w = w.c["w"]
        w.wait("gpsimd", "pw", pw_ready["wo"])
        Gp(lambda E: E.dma_start(
            out=ar(OFF_KVW + 4096, 4096).bitcast(BF16).rearrange("p (k n) -> p k n", k=16),
            in_=d_wsp_lo[:, 24:40, :]), inc="x")
        wo_x = w.c["x"]
        v_x1 = {}
        v_of = {}
        v_aoread = {}
        t_tr_o = {}
        g_on = {}
        v_rq = {}

        def o_head(tt):
            par = tt % 2
            o_norm = ar(OFF_P + 1024 * par, 1024)
            for g2 in range(NG):
                ot = o_tile(tt, g2).rearrange("p (h c) -> p h c", h=8)
                V(lambda E, ot=ot, g2=g2: E.tensor_copy(
                    ST8["ssq"][:, 8 * g2:8 * g2 + 8], ot[:, :, 64:65].squeeze(2)))
            w.drain("vector")
            if tt >= 1:
                w.wait("vector", "g", g_on[tt - 1])
            for g2 in range(NG):
                v_rq[tt] = V(lambda E, g2=g2: E.reciprocal(
                    ST8["rq"][:, 8 * g2:8 * g2 + 8],
                    ST8["ssq"][:, 8 * g2:8 * g2 + 8]),
                    inc="v" if g2 == NG - 1 else None)
            w.wait("gpsimd", "v", v_rq[tt])
            if tt >= 2:
                w.wait("gpsimd", "t", t_tr_o[tt - 2])
            for g2 in range(NG):
                ot = o_tile(tt, g2).rearrange("p (h c) -> p h c", h=8)
                g_on[tt] = w.op("gpsimd", lambda E, ot=ot, g2=g2, o_norm=o_norm:
                                E.tensor_mul(
                    o_norm[:, 512 * g2:512 * g2 + 512].rearrange("p (h c) -> p h c", h=8),
                    ot[:, :, 0:64],
                    ST8["rq"][:, 8 * g2:8 * g2 + 8].unsqueeze(2).to_broadcast([128, 8, 64])),
                    inc="g" if g2 == NG - 1 else None)
            v_of[tt], t_tr_o[tt] = transpose8(
                lambda c, par=par: ar(OFF_P + 1024 * par + 128 * c, 128),
                ar(OFF_KNF + 1024 * par, 1024), 8, t_deps=[("g", g_on[tt])],
                v_extra=([("t", t_opmm[tt - 2])] if tt >= 2 else ()),
                split_to=(bfv(OFF_KNF + 1024 * par, 1024),
                          bfv(OFF_KNF + 1024 * par, 1024, 1024)))
            if tt >= 2:
                w.wait("gpsimd", "v", v_x1[tt - 2])
            Gp(lambda E, tt=tt, par=par: E.dma_start(
                out=ar(OFF_XBLK + 1024 * par, 1024),
                in_=dx_q[128 * tt:128 * tt + 128, :]), inc="x")
            x_cnt[("o", tt)] = w.c["x"]

        t_opmm = {}

        def o_tail(tt):
            par = tt % 2
            ao = ar(OFF_XN + 1024 * par, 1024)
            w.wait("tensor", "v", v_of[tt])
            w.wait("tensor", "w", wo_w)
            w.wait("tensor", "x", wo_x)
            w.wait("tensor", "v", v_aoread.get(tt - 1))
            oh = lambda c, par=par: bfv(OFF_KNF + 1024 * par, 128, 128 * c)
            ol = lambda c, par=par: bfv(OFF_KNF + 1024 * par, 128, 1024 + 128 * c)
            for nh in range(2):
                t_op = mm3(pA[:, 512 * nh:512 * nh + 512], oh, ol,
                           lambda c, nh=nh: bfv(OFF_KVW, 512, 1024 * c + 512 * nh),
                           lambda c, nh=nh: bfv(OFF_KVW, 512, 8192 + 1024 * c + 512 * nh),
                           NCH, inc_last=(nh == 1))
            t_opmm[tt] = t_op
            w.wait("vector", "t", t_op)
            v_aoread[tt] = V(lambda E, ao=ao: E.tensor_add(ao, pA[:, :], sb_bo[:, :]),
                             inc="v")
            ln_chain(par, ao, ao, "mu", "sd", "rs", False,
                     s_dep=[("v", v_aoread[tt])])
            w.wait("vector", "x", x_cnt[("o", tt)])
            w.drain("vector")
            v_x1[tt] = V(lambda E, tt=tt, par=par, ao=ao: E.tensor_add(
                x1_tile(tt), ao, ar(OFF_XBLK + 1024 * par, 1024)), inc="v")

        for tt in range(tt_n):
            o_head(tt)
            if tt >= 1:
                o_tail(tt - 1)
        o_tail(tt_n - 1)

        if stage < 5:
            joins = dict(w.c)
            for sem in ("t", "v", "s"):
                w.wait("gpsimd", sem, joins[sem])
            for name in dumps:
                Gp(lambda E, name=name: E.dma_start(
                    out=dump_drams[name][:, :], in_=dump_specs[name][0]), inc="st")
            w.wait("gpsimd", "st", w.c["st"])
            w.wait("sync", "st", w.c["st"])
            return
        joins = dict(w.c)
        for r in ROLES:
            for sem in ("t", "v", "s"):
                w.wait(r, sem, joins[sem])
        l3_off = {0: OFF_VN, 1: OFF_KTOK}
        bank_off = [
            [OFF_XBLK + 512 * c for c in range(8)],
            [OFF_XNF + 512 * c for c in range(4)] + [OFF_P + 512 * c for c in range(4)],
        ]
        n_tg = max(1, tt_n // 8)
        exp_steps = [(nh, e) for _tg in range(n_tg) for nh in range(2)
                     for e in range(E)]
        step_t = {}
        w_after = {}

        def load_emit(s):
            nh, e = exp_steps[s]
            bk = s % 2
            if s >= 2:
                w.wait("sync", "t", step_t[s - 2])
            if bk == 0:
                Sy(lambda E_, e=e, nh=nh: E_.dma_start(
                    out=ar(OFF_XBLK, 4096).bitcast(BF16),
                    in_=d_ews[e, nh, :, :]), inc="w")
            else:
                Sy(lambda E_, e=e, nh=nh: E_.dma_start(
                    out=ar(OFF_XNF, 2048).bitcast(BF16),
                    in_=d_ews[e, nh, :, 0:4096]), inc="w")
                Sy(lambda E_, e=e, nh=nh: E_.dma_start(
                    out=ar(OFF_P, 2048).bitcast(BF16),
                    in_=d_ews[e, nh, :, 4096:8192]), inc="w")
            w_after[s] = w.c["w"]

        store_cnt = {}
        for tg in range(max(1, tt_n // 8)):
            tiles = list(range(8 * tg, min(8 * tg + 8, tt_n)))
            if tg > 0:
                w.wait("vector", "t", w.c["t"])
                w.wait("vector", "s", w.c["s"])
            v_l3d = {}
            t_l3tr = {}

            def head_steps(i, tt):
                par = tt % 2
                v_pre = [("t", t_l3tr[i - 2])] if i >= 2 else None
                steps, st = ln_chain_steps(par, x1_tile(tt), ar(l3_off[par], 1024),
                                           "mu", "sd", "rs", False, v_pre=v_pre)

                def fin():
                    w.drain("vector")
                    v_l3d[i] = V(lambda E, i=i, par=par: E.tensor_copy(
                        ar(OFF_KVW + 1024 * i, 1024), ar(l3_off[par], 1024)),
                        inc="v")
                return steps, fin

            def ffn_tail(i, tt):
                par = tt % 2
                v_f, t_l3tr[i] = transpose8(
                    lambda c, par=par: ar(l3_off[par] + 128 * c, 128),
                    ar(OFF_QNF + 1024 * i, 1024), 8, t_deps=[("v", v_l3d[i])],
                    split_to=(bfv(OFF_QNF + 1024 * i, 1024),
                              bfv(OFF_QNF + 1024 * i, 1024, 1024)))
                w.wait("tensor", "v", v_f)
                w.wait("tensor", "g", g_gw)
                t_g = mm3(pS0[:, 0:8],
                          lambda c, i=i: bfv(OFF_QNF + 1024 * i, 128, 128 * c),
                          lambda c, i=i: bfv(OFF_QNF + 1024 * i, 128, 1024 + 128 * c),
                          lambda c: sb_gwb[:, 8 * c:8 * c + 8],
                          lambda c: sb_gwb[:, 64 + 8 * c:64 + 8 * c + 8], NCH)
                w.wait("vector", "t", t_g)
                V(lambda E, i=i: E.tensor_copy(sb_sc[:, 8 * i:8 * i + 8], pS0[:, 0:8]),
                  inc="v")

            hsts = {0: head_steps(0, tiles[0])}
            hsts[0][0][0]()
            for i, tt in enumerate(tiles):
                steps, fin = hsts[i]
                steps[1]()
                if i >= 1:
                    ffn_tail(i - 1, tiles[i - 1])
                if i + 1 < len(tiles):
                    hsts[i + 1] = head_steps(i + 1, tiles[i + 1])
                    hsts[i + 1][0][0]()
                steps[2]()
                steps[3]()
                fin()
            ffn_tail(len(tiles) - 1, tiles[-1])
            w.drain("vector")
            for i in range(len(tiles)):
                V(lambda E, i=i: E.tensor_reduce(sb_mx[:, i:i + 1], sb_sc[:, 8 * i:8 * i + 8],
                                                 AX.X, ALU.max))
            w.drain("vector")
            for i in range(len(tiles)):
                v_masks = V(lambda E, i=i: E.tensor_scalar(sb_masks[:, 8 * i:8 * i + 8],
                                                           sb_sc[:, 8 * i:8 * i + 8],
                                                           sb_mx[:, i:i + 1], None,
                                                           ALU.is_equal),
                            inc="v" if i == len(tiles) - 1 else None)
            if tg == 0:
                presplit_flush("ew")
                load_emit(0)
                load_emit(1)
            v_acc = None
            v_acc_par = {}
            for local_s in range(16):
                s = 16 * tg + local_s
                nh, e = exp_steps[s]
                bk = s % 2
                w.wait("tensor", "w", w_after[s])
                w.wait("tensor", "v", v_masks)
                for i in range(len(tiles)):
                    pd = pdst_map[i % 2]
                    w.wait("tensor", "v", v_acc_par.get(i % 2))
                    t_mm = mm3(pd[:, 0:512],
                               lambda c, i=i: bfv(OFF_QNF + 1024 * i, 128, 128 * c),
                               lambda c, i=i: bfv(OFF_QNF + 1024 * i, 128,
                                                  1024 + 128 * c),
                               lambda c, bk=bk: bfv(bank_off[bk][c], 512, 0),
                               lambda c, bk=bk: bfv(bank_off[bk][c], 512, 512),
                               NCH)
                    w.wait("vector", "t", t_mm)
                    v_acc = V(lambda E, i=i, e=e, nh=nh, pd=pd: E.scalar_tensor_tensor(
                        ar(OFF_KVW + 1024 * i + 512 * nh, 512), pd[:, 0:512],
                        sb_masks[:, 8 * i + e:8 * i + e + 1],
                        ar(OFF_KVW + 1024 * i + 512 * nh, 512),
                        ALU.mult, ALU.add), inc="v")
                    v_acc_par[i % 2] = v_acc
                step_t[s] = w.c["t"]
                if s + 2 < 16 * n_tg:
                    load_emit(s + 2)
            w.wait("vector", "v", v_acc)

            def y_steps(i, tt):
                par = tt % 2
                v_pre = ([("st", store_cnt.get(par, 0))]
                         if (tg, i) not in ((0, 0), (0, 1)) else None)
                return ln_chain_steps(par, ar(OFF_KVW + 1024 * i, 1024),
                                      ar(OFF_KNF + 1024 * par, 1024),
                                      "mu", "sd", "rs", False,
                                      s_dep=[("v", w.c["v"])], v_pre=v_pre)

            ysts = {0: y_steps(0, tiles[0])}
            ysts[0][0][0]()
            for i, tt in enumerate(tiles):
                par = tt % 2
                yb = ar(OFF_KNF + 1024 * par, 1024)
                steps, _st = ysts[i]
                steps[1]()
                if i + 1 < len(tiles):
                    ysts[i + 1] = y_steps(i + 1, tiles[i + 1])
                    ysts[i + 1][0][0]()
                steps[2]()
                steps[3]()
                w.drain("vector")
                v_y = V(lambda E, yb=yb, tt=tt: E.tensor_add(yb, yb, x1_tile(tt)), inc="v")
                w.wait("sync", "v", v_y)
                Sy(lambda E, yb=yb, tt=tt: E.dma_start(out=d_y[128 * tt:128 * tt + 128, :],
                                                       in_=yb), inc="st")
                store_cnt[par] = w.c["st"]

        joins = dict(w.c)
        for sem in ("t", "v", "s"):
            w.wait("gpsimd", sem, joins[sem])
        for name in dumps:
            Gp(lambda E, name=name: E.dma_start(
                out=dump_drams[name][:, :], in_=dump_specs[name][0]), inc="sd")
        w.wait("gpsimd", "sd", w.c["sd"])
        w.wait("gpsimd", "st", w.c["st"])
        w.wait("sync", "st", w.c["st"])

    with nc.Block() as block:
        block.sync(lambda E: schedule(Walk(nc, sems, "sync", E)))
        block.gpsimd(lambda E: schedule(Walk(nc, sems, "gpsimd", E)))
        block.tensor(lambda E: schedule(Walk(nc, sems, "tensor", E)))
        block.vector(lambda E: schedule(Walk(nc, sems, "vector", E)))
        block.scalar(lambda E: schedule(Walk(nc, sems, "scalar", E)))
    return nc


_prog_cache = {}


def _get_prog(dumps=(), nb=NB, tt_n=TT, stage=5, qt_mode=2):
    key = (tuple(dumps), nb, tt_n, stage, qt_mode)
    if key not in _prog_cache:
        _prog_cache[key] = build_program(tuple(dumps), nb, tt_n, stage, qt_mode)
    return _prog_cache[key]


def kernel(L0_attn_x, L0_attn_ln_gamma, L0_attn_ln_beta, L0_attn_q_gamma, L0_attn_k_gamma,
           L0_attn_w_q, L0_attn_w_k, L0_attn_w_v, L0_attn_w_o, L0_attn_b_o,
           L0_ffn_gate_w, L0_ffn_expert_w, _dumps=(), _results_out=None, _nb=NB, _tt_n=TT, _stage=5, _qt_mode=2):
    x = np.asarray(L0_attn_x, dtype=np.float32)
    wq = np.ascontiguousarray(np.asarray(L0_attn_w_q, np.float32).T)
    wk = np.ascontiguousarray(np.asarray(L0_attn_w_k, np.float32).T)
    wv = np.ascontiguousarray(np.asarray(L0_attn_w_v, np.float32).T)
    wo = np.ascontiguousarray(np.asarray(L0_attn_w_o, np.float32).T)
    gw = np.ascontiguousarray(np.asarray(L0_ffn_gate_w, np.float32).T)
    ew = np.ascontiguousarray(np.transpose(np.asarray(L0_ffn_expert_w, np.float32), (0, 2, 1)))
    lng = np.ascontiguousarray(np.asarray(L0_attn_ln_gamma, np.float32))
    lnb = np.ascontiguousarray(np.asarray(L0_attn_ln_beta, np.float32))
    bo = np.ascontiguousarray(np.asarray(L0_attn_b_o, np.float32))
    qg8 = np.ascontiguousarray(np.tile(np.asarray(L0_attn_q_gamma, np.float32), 8))
    kg8 = np.ascontiguousarray(np.tile(np.asarray(L0_attn_k_gamma, np.float32), 8))

    nc = _get_prog(tuple(_dumps), _nb, _tt_n, _stage, _qt_mode)
    in_maps = []
    for c in range(8):
        b, sl = c // 4, c % 4
        in_maps.append(dict(
            xq=np.ascontiguousarray(x[b, T * sl:T * (sl + 1)]),
            xkv=np.ascontiguousarray(x[b]),
            wq=wq, wk=wk, wv=wv, wo=wo, lng=lng, lnb=lnb, bo=bo,
            qg8=qg8, kg8=kg8, gw=gw, ew=ew,
        ))
    res = run_bass_kernel_spmd(nc, in_maps, core_ids=list(range(8)))
    if _results_out is not None:
        _results_out.append(res)
    y = np.empty((B, S, D), dtype=np.float32)
    for c in range(8):
        b, sl = c // 4, c % 4
        y[b, T * sl:T * (sl + 1)] = res.results[c]["y"]
    return y


# version 20
# speedup vs baseline: 1.0150x; 1.0150x over previous
import numpy as np
import concourse.bass as bass
import concourse.mybir as mybir
from concourse.bass_utils import run_bass_kernel_spmd

F32 = mybir.dt.float32
BF16 = mybir.dt.bfloat16
AF = mybir.ActivationFunctionType
ALU = mybir.AluOpType
AX = mybir.AxisListType

D = 1024
H = 16
DH = 64
S = 8192
B = 2
T = 2048
TT = T // 128
NB = S // 128
HG = 8
NG = 2
E = 8
NCH = D // 128
LN_EPS = 1e-5
RMS_EPS = 1e-8

ROLES = ("sync", "gpsimd", "tensor", "vector", "scalar")
DMA_SEMS = {"x", "xq", "w", "st", "sd", "pw"}

OFF_QNF = 0
OFF_O = OFF_QNF + 4 * 2048
OFF_KVW = OFF_O + 16 * 1040
OFF_XBLK = OFF_KVW + 8192
OFF_XN = OFF_XBLK + 2048
OFF_XNF = OFF_XN + 2048
OFF_KTOK = OFF_XNF + 2048
OFF_KNF = OFF_KTOK + 1024
OFF_VN = OFF_KNF + 2048
OFF_P = OFF_VN + 1040
OFF_QSQ = OFF_P + 2048
OFF_WSTG = OFF_QSQ + 512
ARENA_N = OFF_WSTG + 1024


class Walk:

    def __init__(self, nc, sems, role, eng):
        self.nc = nc
        self.sems = sems
        self.role = role
        self.eng = eng
        self.c = {k: 0 for k in sems}

    def op(self, role, f, inc=None):
        newv = None
        if inc is not None:
            if inc in DMA_SEMS and self.c[inc] > 0 and role == self.role:
                self.eng.wait_ge(self.sems[inc], self.c[inc])
            self.c[inc] += 16 if inc in DMA_SEMS else 1
            newv = self.c[inc]
        if role == self.role:
            ins = f(self.eng)
            if inc is not None:
                ins.then_inc(self.sems[inc], 16 if inc in DMA_SEMS else 1)
        return newv

    def wait(self, role, sem, val):
        if val is None or val <= 0:
            return
        assert val <= self.c[sem], f"wait {sem}>={val} but only {self.c[sem]} emitted"
        if role == self.role:
            self.eng.wait_ge(self.sems[sem], val)

    def drain(self, role):
        if role == self.role:
            self.eng.drain()


def build_program(dumps=(), nb=NB, tt_n=TT, stage=5, qt_mode=2):
    nc = bass.Bass("TRN2", target_bir_lowering=False)

    dx_q = nc.dram_tensor("xq", [T, D], F32, kind="ExternalInput").ap()
    dx_kv = nc.dram_tensor("xkv", [S, D], F32, kind="ExternalInput").ap()
    d_wq = nc.dram_tensor("wq", [D, D], F32, kind="ExternalInput").ap()
    d_wk = nc.dram_tensor("wk", [D, D], F32, kind="ExternalInput").ap()
    d_wv = nc.dram_tensor("wv", [D, D], F32, kind="ExternalInput").ap()
    d_wo = nc.dram_tensor("wo", [D, D], F32, kind="ExternalInput").ap()
    d_lng = nc.dram_tensor("lng", [D], F32, kind="ExternalInput").ap()
    d_lnb = nc.dram_tensor("lnb", [D], F32, kind="ExternalInput").ap()
    d_bo = nc.dram_tensor("bo", [D], F32, kind="ExternalInput").ap()
    d_qg8 = nc.dram_tensor("qg8", [8 * DH], F32, kind="ExternalInput").ap()
    d_kg8 = nc.dram_tensor("kg8", [8 * DH], F32, kind="ExternalInput").ap()
    d_gw = nc.dram_tensor("gw", [D, E], F32, kind="ExternalInput").ap()
    d_ew = nc.dram_tensor("ew", [E, D, D], F32, kind="ExternalInput").ap()
    d_y = nc.dram_tensor("y", [T, D], F32, kind="ExternalOutput").ap()
    d_ews = nc.dram_tensor("ewsplit", [E, 2, 128, NCH * 1024], BF16, kind="Internal").ap()
    d_wsp_hi = nc.dram_tensor("wsplit_hi", [128, 40, 512], BF16, kind="Internal").ap()
    d_wsp_lo = nc.dram_tensor("wsplit_lo", [128, 40, 512], BF16, kind="Internal").ap()

    idn = nc.inline_tensor(np.eye(128, dtype=np.float32), "idn")
    import ml_dtypes
    idnb = nc.inline_tensor(np.eye(128).astype(ml_dtypes.bfloat16), "idnb")

    AR = nc.alloc_sbuf_tensor("arena", [128, ARENA_N], F32).ap()
    sb_ident = nc.alloc_sbuf_tensor("ident", [128, 128], F32).ap()
    sb_identb = nc.alloc_sbuf_tensor("identb", [128, 128], BF16).ap()
    sb_lng = nc.alloc_sbuf_tensor("lngr", [128, D], F32).ap()
    sb_lnb = nc.alloc_sbuf_tensor("lnbr", [128, D], F32).ap()
    sb_bo = nc.alloc_sbuf_tensor("bor", [128, D], F32).ap()
    sb_qg = nc.alloc_sbuf_tensor("qgr", [128, 512], F32).ap()
    sb_kg = nc.alloc_sbuf_tensor("kgr", [128, 512], F32).ap()
    sb_gw = nc.alloc_sbuf_tensor("gwr", [128, 64], F32).ap()
    sb_gwb = nc.alloc_sbuf_tensor("gwb", [128, 128], BF16).ap()
    sb_mb = nc.alloc_sbuf_tensor("mbias", [128, 1], F32).ap()
    sb_c30 = nc.alloc_sbuf_tensor("c1em30", [128, 1], F32).ap()
    sb_junk = nc.alloc_sbuf_tensor("junk", [128, 4], F32).ap()
    sb_cln8 = nc.alloc_sbuf_tensor("cln8", [128, 1], F32).ap()
    sb_dumpS = nc.alloc_sbuf_tensor("dumpS", [128, D], BF16).ap()
    sb_dumpS2 = nc.alloc_sbuf_tensor("dumpS2", [128, D], BF16).ap()
    sb_masks = nc.alloc_sbuf_tensor("masks", [128, 64], F32).ap()
    sb_sc = nc.alloc_sbuf_tensor("scores", [128, 64], F32).ap()
    sb_mx = nc.alloc_sbuf_tensor("gmax", [128, 8], F32).ap()
    ST = {}
    for nm in ("sx", "sxx", "mu", "m2", "var", "sd", "rs", "mu1", "sd1", "rs1"):
        ST[nm] = nc.alloc_sbuf_tensor("st_" + nm, [128, 2], F32).ap()
    ST8 = {}
    for nm in ("ssq", "nrm", "rq"):
        ST8[nm] = nc.alloc_sbuf_tensor("st8_" + nm, [128, 16], F32).ap()

    pA = nc.alloc_psum_tensor("pA", [128, 1024], F32).ap()
    pB = nc.alloc_psum_tensor("pB", [128, 1024], F32).ap()
    pS0 = nc.alloc_psum_tensor("pS0", [128, 1024], F32).ap()
    pS1 = nc.alloc_psum_tensor("pS1", [128, 1024], F32).ap()
    pS = (pS0, pS1)
    pdst_map = {0: pA, 1: pB}

    sems = {k: nc.alloc_semaphore("sem_" + k) for k in ("x", "xq", "w", "t", "v", "s", "st", "sd", "g", "pw")}

    def ar(off, n):
        return AR[:, off:off + n]

    def qnf(pair):
        return ar(OFF_QNF + 2048 * pair, 2048)

    def o_tile(qt, g):
        return ar(OFF_O + 1040 * qt + 520 * g, 520)

    def x1_tile(tt):
        return ar(OFF_O + 1040 * tt, 1024)

    dump_specs = {
        "qnf": (ar(OFF_QNF, 8192), [128, 8192]),
        "O": (ar(OFF_O, 16640), [128, 16640]),
        "x1": (ar(OFF_O, 16640), [128, 16640]),
        "masks": (sb_masks[:, :], [128, 64]),
        "scores": (sb_sc[:, :], [128, 64]),
        "mbias": (sb_mb[:, :], [128, 1]),
        "knf": (ar(OFF_KNF, 2048), [128, 2048]),
        "vn": (ar(OFF_VN, 1040), [128, 1040]),
        "P": (ar(OFF_P, 2048), [128, 2048]),
    }
    dump_drams = {name: nc.dram_tensor("dump_" + name, dump_specs[name][1], F32,
                                       kind="ExternalOutput").ap()
                  for name in dumps}

    def schedule(w: Walk):
        def V(f, inc=None):
            return w.op("vector", f, inc)

        def Sc(f, inc=None):
            return w.op("scalar", f, inc)

        def Tn(f, inc=None):
            return w.op("tensor", f, inc)

        def Sy(f, inc=None):
            return w.op("sync", f, inc)

        def Gp(f, inc=None):
            return w.op("gpsimd", f, inc)

        guard_v = {}

        if dumps:
            V(lambda E: E.memset(AR[:, :], 0.0), inc="v")
            v_z = w.c["v"]
            w.wait("sync", "v", v_z)
            w.wait("gpsimd", "v", v_z)
            w.wait("scalar", "v", v_z)
            w.wait("tensor", "v", v_z)
        Sy(lambda E: E.dma_start(out=sb_lng[:, :], in_=d_lng[None, :].partition_broadcast(128)), inc="w")
        Sy(lambda E: E.dma_start(out=sb_lnb[:, :], in_=d_lnb[None, :].partition_broadcast(128)), inc="w")
        Sy(lambda E: E.dma_start(out=sb_bo[:, :], in_=d_bo[None, :].partition_broadcast(128)), inc="w")
        Sy(lambda E: E.dma_start(out=sb_qg[:, :], in_=d_qg8[None, :].partition_broadcast(128)), inc="w")
        Sy(lambda E: E.dma_start(out=sb_kg[:, :], in_=d_kg8[None, :].partition_broadcast(128)), inc="w")
        Sy(lambda E: E.dma_start(out=sb_ident[:, :], in_=idn.ap()[:, :]), inc="w")
        Sy(lambda E: E.dma_start(out=sb_identb[:, :], in_=idnb.ap()[:, :]), inc="w")
        for c in range(NCH):
            Sy(lambda E, c=c: E.dma_start(out=sb_gw[:, 8 * c:8 * c + 8],
                                          in_=d_gw[128 * c:128 * c + 128, :]), inc="w")
        w_params = w.c["w"]

        w.drain("vector")
        V(lambda E: E.memset(sb_c30[:, :], 1e-30))
        V(lambda E: E.memset(sb_cln8[:, :], 2.0794415416798357))
        V(lambda E: E.memset(ar(OFF_KNF, 2048), 0.0))
        w.wait("vector", "w", w_params)
        V(lambda E: E.tensor_reduce(ST["sx"][:, 0:1], sb_qg[:, 0:64], AX.X, ALU.max,
                                    apply_absolute_value=True))
        V(lambda E: E.tensor_reduce(ST["sxx"][:, 0:1], sb_kg[:, 0:64], AX.X, ALU.max,
                                    apply_absolute_value=True))
        w.drain("vector")
        V(lambda E: E.tensor_mul(sb_mb[:, :], ST["sx"][:, 0:1], ST["sxx"][:, 0:1]))
        w.drain("vector")
        V(lambda E: E.tensor_scalar(sb_mb[:, :], sb_mb[:, :], -8.0, None, ALU.mult))
        V(lambda E: E.tensor_scalar(sb_qg[:, :], sb_qg[:, :], 0.125, None, ALU.mult), inc="v")
        v_prologue = w.c["v"]
        w.wait("scalar", "v", v_prologue)

        def ln_chain_steps(par, x_ap, out_ap, mu_k, sd_k, rs_k, affine, s_dep=None,
                           v_pre=None):
            st = {}

            def s_sums():
                w.wait("scalar", "v", guard_v.get(("ln", par), 0))
                for sem, val in (s_dep or []):
                    w.wait("scalar", sem, val)
                Sc(lambda E: E.activation(sb_dumpS[:, :], x_ap, AF.Copy,
                                          accum_out=ST["sx"][:, par:par + 1]))
                Sc(lambda E: E.activation(sb_dumpS2[:, :], x_ap, AF.Square,
                                          accum_out=ST["sxx"][:, par:par + 1]), inc="s")
                st["sums"] = w.c["s"]
                w.wait("scalar", "s", st["sums"])
                Sc(lambda E: E.activation(ST[mu_k][:, par:par + 1],
                                          ST["sx"][:, par:par + 1], AF.Copy,
                                          scale=1.0 / D))
                Sc(lambda E: E.activation(ST[sd_k][:, par:par + 1],
                                          ST["sx"][:, par:par + 1], AF.Copy,
                                          scale=-1.0 / D))
                Sc(lambda E: E.activation(ST["m2"][:, par:par + 1],
                                          ST["sxx"][:, par:par + 1], AF.Copy,
                                          bias=LN_EPS, scale=1.0 / D), inc="s")
                st["stats"] = w.c["s"]

            def s_var():
                w.wait("vector", "s", st["stats"])
                V(lambda E: E.scalar_tensor_tensor(ST["var"][:, par:par + 1],
                                                   ST[mu_k][:, par:par + 1],
                                                   ST[sd_k][:, par:par + 1],
                                                   ST["m2"][:, par:par + 1],
                                                   ALU.mult, ALU.add), inc="v")
                st["var"] = w.c["v"]

            def s_rsqrt():
                w.wait("scalar", "v", st["var"])
                Sc(lambda E: E.activation(ST["sxx"][:, par:par + 1],
                                          ST["var"][:, par:par + 1], AF.Ln), inc="s")
                w.wait("scalar", "s", w.c["s"])
                Sc(lambda E: E.activation(ST[rs_k][:, par:par + 1],
                                          ST["sxx"][:, par:par + 1], AF.Exp,
                                          scale=-0.5), inc="s")
                st["rs"] = w.c["s"]

            def s_apply():
                w.wait("vector", "s", st["rs"])
                if v_pre:
                    for sem, val in v_pre:
                        w.wait("vector", sem, val)
                if affine:
                    V(lambda E: E.tensor_scalar(out_ap, x_ap, ST[mu_k][:, par:par + 1],
                                                ST[rs_k][:, par:par + 1],
                                                ALU.subtract, ALU.mult))
                    w.drain("vector")
                    V(lambda E: E.tensor_mul(out_ap, out_ap, sb_lng[:, :]))
                    w.drain("vector")
                    v_done = V(lambda E: E.tensor_add(out_ap, out_ap, sb_lnb[:, :]),
                               inc="v")
                else:
                    v_done = V(lambda E: E.tensor_scalar(out_ap, x_ap,
                                                         ST[mu_k][:, par:par + 1],
                                                         ST[rs_k][:, par:par + 1],
                                                         ALU.subtract, ALU.mult),
                               inc="v")
                guard_v[("ln", par)] = v_done
                st["done"] = v_done
                return v_done

            return [s_sums, s_var, s_rsqrt, s_apply], st

        def ln_chain(par, x_ap, out_ap, mu_k, sd_k, rs_k, affine, s_dep=None,
                     v_pre=None):
            steps, st = ln_chain_steps(par, x_ap, out_ap, mu_k, sd_k, rs_k, affine,
                                       s_dep, v_pre)
            for s in steps:
                s()
            return st["done"]

        def bfv(off_f32, n_bf16, start_bf16=0):
            n_f32 = (start_bf16 + n_bf16 + 1) // 2
            return ar(off_f32, n_f32).bitcast(BF16)[:, start_bf16:start_bf16 + n_bf16]

        def gp_split(dst_hi, dst_lo, src_f32):
            w.op("gpsimd", lambda E: E.tensor_copy(dst_hi, src_f32), inc="g")
            w.wait("gpsimd", "g", w.c["g"])
            return w.op("gpsimd", lambda E: E.tensor_sub(dst_lo, src_f32, dst_hi),
                        inc="g")

        def load_split(dram_chunk_fn, nchunk, width, dst_hi_fn, dst_lo_fn):
            g_done = None
            for c in range(nchunk):
                stg = ar(OFF_WSTG, width)
                w.wait("sync", "g", g_done if c >= 1 else w.c["g"])
                Sy(lambda E, c=c, stg=stg: E.dma_start(out=stg, in_=dram_chunk_fn(c)),
                   inc="w")
                w.wait("gpsimd", "w", w.c["w"])
                g_done = gp_split(dst_hi_fn(c), dst_lo_fn(c), stg)
            return g_done

        def load_split_vec(dram_chunk_fn, nchunk, width, dst_hi_fn, dst_lo_fn):
            vd = {}
            for c in range(nchunk):
                stg = ar(OFF_WSTG + 512 * (c % 2), width)
                w.wait("sync", "v", vd.get(c - 2, w.c["v"] if c < 2 else 0))
                Sy(lambda E, c=c, stg=stg: E.dma_start(out=stg, in_=dram_chunk_fn(c)),
                   inc="w")
                w.wait("vector", "w", w.c["w"])
                V(lambda E, c=c, stg=stg: E.tensor_copy(dst_hi_fn(c), stg), inc="v")
                w.wait("vector", "v", w.c["v"])
                vd[c] = V(lambda E, c=c, stg=stg: E.tensor_sub(dst_lo_fn(c), stg,
                                                               dst_hi_fn(c)), inc="v")
            return vd[nchunk - 1]

        def load_split_gpself(dram_chunk_fn, nchunk, dst_hi_fn, dst_lo_fn, v_pre):
            w.wait("gpsimd", "v", v_pre)
            for c in range(nchunk):
                stg = ar(OFF_WSTG, 512)
                w.wait("gpsimd", "g", w.c["g"])
                Gp(lambda E, c=c, stg=stg: E.dma_start(out=stg, in_=dram_chunk_fn(c)),
                   inc="pw")
                w.wait("gpsimd", "pw", w.c["pw"])
                gp_split(dst_hi_fn(c), dst_lo_fn(c), stg)
            return w.c["g"]

        def mm3(psum_ap, lhs_hi_fn, lhs_lo_fn, rhs_hi_fn, rhs_lo_fn, nchunk,
                inc_last=True, lo_wait=None, pass1_wait=None):
            last = None
            n = 0
            total = nchunk * 3
            for p, (lf, rf) in enumerate(((lhs_hi_fn, rhs_hi_fn),
                                          (lhs_hi_fn, rhs_lo_fn),
                                          (lhs_lo_fn, rhs_hi_fn))):
                if p == 2 and lo_wait is not None:
                    w.wait("tensor", lo_wait[0], lo_wait[1])
                for c in range(nchunk):
                    if p == 0 and pass1_wait is not None and c == pass1_wait[0]:
                        w.wait("tensor", pass1_wait[1], pass1_wait[2])
                    n += 1
                    last = Tn(lambda E, c=c, lf=lf, rf=rf, first=(n == 1),
                              stop=(n == total): E.matmul(
                                  psum_ap, lf(c), rf(c), start=first, stop=stop),
                              inc="t" if (inc_last and n == total) else None)
            return last

        def transpose8(src_fn, dst_ap, nchunk, t_deps=(), v_extra=(), split_to=None,
                       staged=False):
            for sem, val in t_deps:
                w.wait("tensor", sem, val)
            if staged:
                t_h = None
                for c in range(nchunk):
                    t_h = Tn(lambda E, c=c: E.transpose(pB[:, 128 * c:128 * c + 128],
                                                        src_fn(c), sb_ident[:, :]),
                             inc="t" if c in (3, nchunk - 1) else None)
                    if c == 3:
                        t_half = t_h
                t_tr = t_h
                hi_ap, lo_ap = split_to
                w.wait("vector", "t", t_half)
                for sem, val in v_extra:
                    w.wait("vector", sem, val)
                v_hi1 = V(lambda E: E.tensor_copy(hi_ap[:, 0:512], pB[:, 0:512]),
                          inc="v")
                w.wait("vector", "t", t_tr)
                v_hi2 = V(lambda E: E.tensor_copy(hi_ap[:, 512:1024], pB[:, 512:1024]),
                          inc="v")
                w.wait("vector", "v", v_hi2)
                done = V(lambda E: E.tensor_sub(lo_ap[:, 0:1024], pB[:, 0:1024],
                                                hi_ap[:, 0:1024]), inc="v")
                transpose8.v_hi = v_hi1
                transpose8.v_hi2 = v_hi2
                return done, t_tr
            t_tr = None
            for c in range(nchunk):
                t_tr = Tn(lambda E, c=c: E.transpose(pB[:, 128 * c:128 * c + 128],
                                                     src_fn(c), sb_ident[:, :]),
                          inc="t" if c == nchunk - 1 else None)
            w.wait("vector", "t", t_tr)
            for sem, val in v_extra:
                w.wait("vector", sem, val)
            done = None
            if split_to is None:
                for h in range((nchunk + 3) // 4):
                    n = min(4, nchunk - 4 * h) * 128
                    done = V(lambda E, h=h, n=n: E.tensor_copy(
                        dst_ap[:, 512 * h:512 * h + n], pB[:, 512 * h:512 * h + n]),
                        inc="v")
                transpose8.v_hi = done
            else:
                hi_ap, lo_ap = split_to
                nb_ = nchunk * 128
                v_hi = V(lambda E: E.tensor_copy(hi_ap[:, 0:nb_], pB[:, 0:nb_]),
                         inc="v")
                w.wait("vector", "v", v_hi)
                done = V(lambda E: E.tensor_sub(lo_ap[:, 0:nb_], pB[:, 0:nb_],
                                                hi_ap[:, 0:nb_]), inc="v")
                transpose8.v_hi = v_hi
            return done, t_tr

        def rms_chain(par, psum_ap, out_ap, g_rep, t_dep):
            w.wait("scalar", "t", t_dep)
            w.wait("scalar", "v", guard_v.get(("qsq",), 0))
            Sc(lambda E: E.activation(ar(OFF_QSQ, 512), psum_ap, AF.Square), inc="s")
            s_sq = w.c["s"]
            w.wait("vector", "s", s_sq)
            V(lambda E: E.tensor_reduce(ST8["ssq"][:, 8 * par:8 * par + 8],
                                        ar(OFF_QSQ, 512).rearrange("p (h c) -> p h c", h=8),
                                        AX.X, ALU.add), inc="v")
            guard_v[("qsq",)] = w.c["v"]
            v_ssq = w.c["v"]
            w.wait("scalar", "v", max(v_ssq, guard_v.get(("rms", par), 0)))
            Sc(lambda E: E.activation(ST8["nrm"][:, 8 * par:8 * par + 8],
                                      ST8["ssq"][:, 8 * par:8 * par + 8],
                                      AF.Ln, bias=sb_c30[:, 0:1]), inc="s")
            w.wait("scalar", "s", w.c["s"])
            Sc(lambda E: E.activation(ST8["rq"][:, 8 * par:8 * par + 8],
                                      ST8["nrm"][:, 8 * par:8 * par + 8],
                                      AF.Exp, scale=-0.5, bias=sb_cln8[:, 0:1]), inc="s")
            s_rq = w.c["s"]
            w.wait("vector", "s", s_rq)
            V(lambda E: E.tensor_scalar(ST8["rq"][:, 8 * par:8 * par + 8],
                                        ST8["rq"][:, 8 * par:8 * par + 8],
                                        1e8, None, ALU.min))
            w.drain("vector")
            V(lambda E: E.tensor_mul(
                out_ap.rearrange("p (h c) -> p h c", h=8),
                psum_ap.rearrange("p (h c) -> p h c", h=8),
                ST8["rq"][:, 8 * par:8 * par + 8].unsqueeze(2).to_broadcast([128, 8, 64])))
            w.drain("vector")
            v_done = V(lambda E: E.tensor_mul(out_ap, out_ap, g_rep), inc="v")
            guard_v[("rms", par)] = v_done
            return v_done

        evac_done = {}
        exp_done = {}
        pv_done = {}
        last_exp = {}
        last_pv = {}
        last_evac = {}
        blk_last_pv = {}
        sim_last = {}
        t_kv_done = {}
        t_kntr = {}
        t_xntr = {}
        xn_ready = {}
        qproj_t = {}
        x_cnt = {}

        w.wait("gpsimd", "w", w_params)
        gp_split(sb_gwb[:, 0:64], sb_gwb[:, 64:128], sb_gw[:, :])
        g_gw = w.c["g"]

        presplit_done = [0]
        pw_ready = {}
        presplit_work = []
        for widx, dw in ((0, d_wq), (1, d_wk), (2, d_wv)):
            for c in range(NCH):
                presplit_work.append((
                    lambda c=c, dw=dw: dw[128 * c:128 * c + 128, 512:1024],
                    d_wsp_hi[:, 8 * widx + c, :], d_wsp_lo[:, 8 * widx + c, :],
                    ("wq1", "wk1", "wv1")[widx] if c == NCH - 1 else None))
        for c in range(NCH):
            for half in range(2):
                p = 24 + 2 * c + half
                presplit_work.append((
                    lambda c=c, half=half: d_wo[128 * c:128 * c + 128,
                                                512 * half:512 * half + 512],
                    d_wsp_hi[:, p, :], d_wsp_lo[:, p, :],
                    "wo" if p == 39 else None))
        for e in range(E):
            for c in range(NCH):
                for h in range(2):
                    presplit_work.append((
                        lambda e=e, c=c, h=h: d_ew[e, 128 * c:128 * c + 128,
                                                   512 * h:512 * h + 512],
                        d_ews[e, h, :, 1024 * c:1024 * c + 512],
                        d_ews[e, h, :, 1024 * c + 512:1024 * c + 1024],
                        "ew" if (e, c, h) == (E - 1, NCH - 1, 1) else None))

        def presplit_step():
            k = presplit_done[0]
            if k >= len(presplit_work):
                return
            presplit_done[0] += 1
            src_fn, dst_hi, dst_lo, tag = presplit_work[k]
            stg = ar(OFF_WSTG, 512)
            oh = bfv(OFF_WSTG + 512, 512)
            olo = bfv(OFF_WSTG + 512, 512, 512)
            w.wait("gpsimd", "g", w.c["g"])
            w.wait("gpsimd", "pw", w.c["pw"])
            w.wait("gpsimd", "w", w.c["w"])
            Gp(lambda E_, src_fn=src_fn, stg=stg: E_.dma_start(
                out=stg, in_=src_fn()), inc="pw")
            w.wait("gpsimd", "pw", w.c["pw"])
            gp_split(oh, olo, stg)
            Gp(lambda E_, dst_hi=dst_hi, oh=oh: E_.dma_start(
                out=dst_hi, in_=oh), inc="pw")
            Gp(lambda E_, dst_lo=dst_lo, olo=olo: E_.dma_start(
                out=dst_lo, in_=olo), inc="pw")
            if tag is not None:
                pw_ready[tag] = w.c["pw"]

        def presplit_flush(tag):
            while tag not in pw_ready:
                presplit_step()
            w.wait("sync", "pw", pw_ready[tag])

        for g in (range(NG) if stage >= 2 else ()):
            if g > 0:
                w.wait("sync", "t", t_kv_done.get((g - 1, nb - 1), w.c["t"]))
                w.wait("tensor", "v", w.c["v"])
            w.wait("sync", "g", w.c["g"])

            def q_dma(tt):
                par = tt % 2
                if tt >= 2:
                    w.wait("sync", "t", qproj_t[(g, tt - 2)])
                elif g > 0:
                    w.wait("sync", "v",
                           xn_ready.get((g - 1, nb - 2 + par), w.c["v"]))
                Sy(lambda E, tt=tt, par=par: E.dma_start(
                    out=ar(OFF_XBLK + 1024 * par, 1024),
                    in_=dx_q[128 * tt:128 * tt + 128, :]), inc="xq")
                x_cnt[("q", g, tt)] = w.c["xq"]

            if g == 0:
                q_dma(0)
                if tt_n > 1:
                    q_dma(1)
            if g == 0:
                load_split_vec(lambda c: d_wq[128 * c:128 * c + 128, 0:512],
                               NCH, 512,
                               lambda c: bfv(OFF_KVW, 512, 512 * c),
                               lambda c: bfv(OFF_KVW, 512, 4096 + 512 * c))
                wq_dep = ("v", w.c["v"])
                wk_stage_v = load_split_vec(
                    lambda c: d_wk[128 * c:128 * c + 128, 0:512],
                    NCH, 512,
                    lambda c: bfv(OFF_O, 512, 512 * c),
                    lambda c: bfv(OFF_O, 512, 4096 + 512 * c))
                g_wv0 = load_split_gpself(
                    lambda c: d_wv[128 * c:128 * c + 128, 0:512],
                    NCH,
                    lambda c: bfv(OFF_KVW, 512, 8192 + 512 * c),
                    lambda c: bfv(OFF_KVW, 512, 12288 + 512 * c), wk_stage_v)
            else:
                presplit_flush("wq1")
                Sy(lambda E: E.dma_start(
                    out=ar(OFF_KVW, 2048).bitcast(BF16).rearrange("p (c n) -> p c n", c=8),
                    in_=d_wsp_hi[:, 0:8, :]), inc="w")
                Sy(lambda E: E.dma_start(
                    out=ar(OFF_KVW + 2048, 2048).bitcast(BF16).rearrange("p (c n) -> p c n", c=8),
                    in_=d_wsp_lo[:, 0:8, :]), inc="w")
                wq_dep = ("w", w.c["w"])
                presplit_flush("wv1")
                Sy(lambda E: E.dma_start(
                    out=ar(OFF_KVW + 4096, 2048).bitcast(BF16).rearrange("p (c n) -> p c n", c=8),
                    in_=d_wsp_hi[:, 16:24, :]), inc="w")
                Sy(lambda E: E.dma_start(
                    out=ar(OFF_KVW + 6144, 2048).bitcast(BF16).rearrange("p (c n) -> p c n", c=8),
                    in_=d_wsp_lo[:, 16:24, :]), inc="w")

            def q_ln_steps(tt):
                par = tt % 2
                xb = ar(OFF_XBLK + 1024 * par, 1024)
                xn = ar(OFF_XN + 1024 * par, 1024)
                if tt >= 2:
                    v_pre = [("t", qproj_t[(g, tt - 2)])]
                elif g > 0:
                    v_pre = [("t", t_xntr.get((g - 1, nb - 2 + par), w.c["t"]))]
                else:
                    v_pre = None
                steps0, st0 = ln_chain_steps(par, xb, xn, "mu", "sd", "rs", False,
                                             s_dep=[("xq", x_cnt[("q", g, tt)])],
                                             v_pre=v_pre)

                def s5():
                    steps1, st1 = ln_chain_steps(par, xn, xn, "mu1", "sd1", "rs1",
                                                 True, s_dep=[("v", st0["done"])])
                    st0["steps1"] = steps1
                    st0["st1"] = st1
                    steps1[0]()

                def mk(i2):
                    def f():
                        st0["steps1"][i2]()
                        if i2 == 3:
                            q_xn_ready[tt] = st0["st1"]["done"]
                    return f

                return steps0 + [s5, mk(1), mk(2), mk(3)]

            q_xn_ready = {}
            q_evac = {}
            q_pending = []
            if g > 0:
                q_dma(0)
            for s_ in q_ln_steps(0):
                s_()
            if tt_n > 1:
                if g > 0:
                    q_dma(1)
                q_pending[:] = q_ln_steps(1)
            qn_tr_cnt = {}
            v_xnf_prev = last_evac.get(1, 0)
            defer_qn = None

            def make_qn(tt, par, v_stg):
                def f():
                    w.wait("tensor", "v", v_stg)
                    w.wait("tensor", "v", q_evac.get(tt - 2, 0))
                    pS1b = pS1.bitcast(BF16)
                    for h2 in range(8):
                        t_qtr = Tn(lambda E, h2=h2, par=par: E.transpose(
                            pS1b[:, 1024 * par + 128 * h2:1024 * par + 128 * h2 + 128],
                            ar(OFF_P + 512 * par, 512).bitcast(BF16)[:, 128 * h2:128 * h2 + 128],
                            sb_identb[:, :]), inc="t" if h2 == 7 else None)
                    qn_tr_cnt[tt] = w.c["t"]
                    w.wait("vector", "t", qn_tr_cnt[tt])
                    for h2 in range(8):
                        V(lambda E, h2=h2, tt=tt, par=par: E.tensor_copy(
                            bfv(OFF_QNF, 128, 2048 * h2 + 128 * tt),
                            pS1.bitcast(BF16)[:, 1024 * par + 128 * h2:1024 * par + 128 * h2 + 128]),
                          inc="v" if h2 == 7 else None)
                    q_evac[tt] = w.c["v"]
                return f

            for tt in range(tt_n):
                par = tt % 2
                xnf = ar(OFF_XNF + 1024 * par, 1024)
                v_xn = q_xn_ready[tt]
                tdeps = [("v", v_xn), ("v", v_xnf_prev)]
                v_xnf, _ = transpose8(lambda c, par=par: ar(OFF_XN + 1024 * par + 128 * c, 128),
                                      xnf, 8, t_deps=tdeps,
                                      split_to=(bfv(OFF_XNF + 1024 * par, 1024),
                                                bfv(OFF_XNF + 1024 * par, 1024, 1024)),
                                      staged=True)
                v_xnf_prev = v_xnf
                v_hi1, v_hi2 = transpose8.v_hi, transpose8.v_hi2
                for _ in range(3):
                    if q_pending:
                        q_pending.pop(0)()
                w.wait("tensor", "v", v_hi1)
                w.wait("tensor", "v", guard_v.get(("rms", par), 0))
                w.wait("tensor", "s", last_exp.get(par, 0))
                w.wait("tensor", *wq_dep)
                qproj_t[(g, tt)] = mm3(
                    pS0[:, 512 * par:512 * par + 512],
                    lambda c, par=par: bfv(OFF_XNF + 1024 * par, 128, 128 * c),
                    lambda c, par=par: bfv(OFF_XNF + 1024 * par, 128, 1024 + 128 * c),
                    lambda c: bfv(OFF_KVW, 512, 512 * c),
                    lambda c: bfv(OFF_KVW, 512, 4096 + 512 * c), NCH,
                    lo_wait=("v", v_xnf), pass1_wait=(4, "v", v_hi2))
                if defer_qn is not None:
                    defer_qn()
                if tt + 2 < tt_n:
                    q_dma(tt + 2)
                while q_pending:
                    q_pending.pop(0)()
                w.wait("vector", "t", qproj_t[(g, tt)])
                qn_tok = ar(OFF_KTOK + 512 * par, 512)
                rms_chain(par, pS0[:, 512 * par:512 * par + 512], qn_tok,
                          sb_qg[:, :], w.c["t"])
                stg_b = ar(OFF_P + 512 * par, 512).bitcast(BF16).rearrange(
                    "p (h c) -> p h c", h=8)
                qt3 = qn_tok.rearrange("p (h c) -> p h c", h=8)
                w.wait("vector", "v", w.c["v"])
                w.wait("vector", "t", qn_tr_cnt.get(tt - 2, 0))
                V(lambda E, stg_b=stg_b, qt3=qt3: E.tensor_copy(stg_b[:, :, 0:64], qt3),
                  inc="v")
                w.wait("vector", "v", w.c["v"])
                v_stg = V(lambda E, stg_b=stg_b, qt3=qt3: E.tensor_sub(
                    stg_b[:, :, 64:128], qt3, stg_b[:, :, 0:64]), inc="v")
                defer_qn = make_qn(tt, par, v_stg)
                if tt + 2 < tt_n:
                    q_pending[:] = q_ln_steps(tt + 2)
            defer_qn()
            v_qnf = w.c["v"]

            w.wait("sync", "t", w.c["t"])
            w.wait("sync", "g", w.c["g"])
            if g == 0:
                w.wait("vector", "t", w.c["t"])
                V(lambda E: E.tensor_copy(bfv(OFF_KVW, 4096),
                                          bfv(OFF_O, 4096)), inc="v")
                V(lambda E: E.tensor_copy(bfv(OFF_KVW, 4096, 4096),
                                          bfv(OFF_O, 4096, 4096)), inc="v")
                g_kv_dep = [("v", w.c["v"]), ("g", g_wv0)]
            else:
                presplit_flush("wk1")
                Sy(lambda E: E.dma_start(
                    out=ar(OFF_KVW, 2048).bitcast(BF16).rearrange("p (c n) -> p c n", c=8),
                    in_=d_wsp_hi[:, 8:16, :]), inc="w")
                Sy(lambda E: E.dma_start(
                    out=ar(OFF_KVW + 2048, 2048).bitcast(BF16).rearrange("p (c n) -> p c n", c=8),
                    in_=d_wsp_lo[:, 8:16, :]), inc="w")
                g_kv_dep = [("w", w.c["w"])]

            def emit_xdma(jj):
                par2 = jj % 2
                if jj >= 2:
                    w.wait("gpsimd", "v", xn_ready[(g, jj - 2)])
                else:
                    w.wait("gpsimd", "v",
                           q_xn_ready.get(tt_n - 2 + par2, w.c["v"]))
                Gp(lambda E, jj=jj, par2=par2: E.dma_start(
                    out=ar(OFF_XBLK + 1024 * par2, 1024),
                    in_=dx_kv[128 * jj:128 * jj + 128, :]), inc="x")
                x_cnt[("k", g, jj)] = w.c["x"]

            def ln_step_list(jj):
                par2 = jj % 2
                xb2 = ar(OFF_XBLK + 1024 * par2, 1024)
                xn2 = ar(OFF_XN + 1024 * par2, 1024)
                v_pre2 = ([("t", t_xntr[(g, jj - 2)])] if jj >= 2
                          else [("t", w.c["t"])])
                steps0, st0 = ln_chain_steps(par2, xb2, xn2, "mu", "sd", "rs", False,
                                             s_dep=[("x", x_cnt[("k", g, jj)])],
                                             v_pre=v_pre2)
                def s5():
                    steps1, st1 = ln_chain_steps(par2, xn2, xn2, "mu1", "sd1", "rs1",
                                                 True, s_dep=[("v", st0["done"])])
                    st0["steps1"] = steps1
                    st0["st1"] = st1
                    steps1[0]()

                def mk(i):
                    def f():
                        st0["steps1"][i]()
                        if i == 3:
                            xn_ready[(g, jj)] = st0["st1"]["done"]
                    return f

                return steps0 + [s5, mk(1), mk(2), mk(3)]

            def emit_ln(jj):
                for s in ln_step_list(jj):
                    s()

            ln_slot = 6 if tt_n >= 14 else None
            pending_ln = []
            w.wait("vector", "t", w.c["t"])
            for par2 in range(2):
                V(lambda E, par2=par2: E.memset(
                    ar(OFF_VN + 520 * par2, 520).rearrange(
                        "p (h c) -> p h c", h=8)[:, :, 64:65], 1.0), inc="v")
            pending_ln = []
            emit_xdma(0)
            emit_ln(0)
            if nb > 1:
                emit_xdma(1)
                if ln_slot is None:
                    pass
            for j in (range(nb) if stage >= 3 else ()):
                par = j % 2
                xnf = ar(OFF_XNF + 1024 * par, 1024)
                ktok = ar(OFF_KTOK + 512 * par, 512)
                knf_base = OFF_KNF + 1024 * par
                vn = ar(OFF_VN + 520 * par, 520)
                if ln_slot is None and j >= 1:
                    emit_ln(j)
                tdeps = [("v", xn_ready[(g, j)]), ("v", last_evac.get(1))]
                v_extra = [("t", t_kv_done[(g, j - 2)])] if j >= 2 else ()
                v_xnf, t_xntr[(g, j)] = transpose8(
                    lambda c, par=par: ar(OFF_XN + 1024 * par + 128 * c, 128),
                    xnf, 8, t_deps=tdeps, v_extra=v_extra,
                    split_to=(bfv(OFF_XNF + 1024 * par, 1024),
                              bfv(OFF_XNF + 1024 * par, 1024, 1024)), staged=True)
                v_xnf_hi = transpose8.v_hi
                w.wait("tensor", "v", v_xnf_hi)
                for dep_ in g_kv_dep:
                    w.wait("tensor", *dep_)
                w.wait("tensor", "v", last_evac.get(0))
                xh = lambda c, par=par: bfv(OFF_XNF + 1024 * par, 128, 128 * c)
                xl = lambda c, par=par: bfv(OFF_XNF + 1024 * par, 128, 1024 + 128 * c)
                t_k_done = mm3(pA[:, 0:512], xh, xl,
                               lambda c: bfv(OFF_KVW, 512, 512 * c),
                               lambda c: bfv(OFF_KVW, 512, 4096 + 512 * c), NCH,
                               lo_wait=("v", v_xnf),
                               pass1_wait=(4, "v", transpose8.v_hi2))
                t_kv_done[(g, j)] = mm3(pA[:, 512:1024], xh, xl,
                                        lambda c: bfv(OFF_KVW, 512, 8192 + 512 * c),
                                        lambda c: bfv(OFF_KVW, 512, 12288 + 512 * c), NCH)
                w.wait("vector", "t", t_k_done)
                if j >= 1:
                    w.wait("vector", "t", t_kntr[(g, j - 1)])
                rms_chain(par, pA[:, 0:512], ktok, sb_kg[:, :], t_k_done)
                stgA = ar(OFF_QSQ, 512).bitcast(BF16).rearrange("p (h c) -> p h c", h=8)
                stgB = ar(OFF_KTOK + 512 * (1 - par), 512).bitcast(BF16).rearrange(
                    "p (h c) -> p h c", h=8)
                kt3 = ktok.rearrange("p (h c) -> p h c", h=8)
                w.wait("vector", "v", w.c["v"])
                if j >= 1:
                    w.wait("vector", "t", t_kntr[(g, j - 1)])
                V(lambda E, stgA=stgA, kt3=kt3: E.tensor_copy(stgA[:, :, 0:64], kt3),
                  inc="v")
                w.wait("vector", "v", w.c["v"])
                v_stgA = V(lambda E, stgA=stgA, kt3=kt3: E.tensor_sub(
                    stgA[:, :, 64:128], kt3, stgA[:, :, 0:64]), inc="v")
                w.wait("vector", "v", v_stgA)
                srcswap = ar(OFF_QSQ, 512).bitcast(BF16).rearrange(
                    "p (h s c) -> p h s c", h=8, s=2)[:, :, ::-1, :]
                v_stgk = V(lambda E, stgB=stgB, srcswap=srcswap: E.tensor_copy(
                    stgB.rearrange("p h (s c) -> p h s c", s=2), srcswap), inc="v")
                guard_v[("qsq",)] = v_stgk
                w.wait("vector", "t", t_kv_done[(g, j)])
                if j >= 2:
                    w.wait("vector", "t", blk_last_pv.get((g, j - 2)))
                v_vn = V(lambda E, par=par: E.tensor_copy(
                    ar(OFF_VN + 520 * par, 520).rearrange("p (h c) -> p h c", h=8)[:, :, 0:64],
                    pA[:, 512:1024].rearrange("p (h c) -> p h c", h=8)), inc="v")
                v_ktok = w.c["v"]
                w.wait("tensor", "v", v_stgA)
                if j >= 2 and qt_mode >= 1:
                    w.wait("tensor", "t", sim_last.get((g, j - 2)))
                pBb = pB.bitcast(BF16)
                for h in range(HG):
                    t_kntrA = Tn(lambda E, h=h, par=par: E.transpose(
                        pBb[:, 128 * h:128 * h + 128],
                        ar(OFF_QSQ, 512).bitcast(BF16)[:, 128 * h:128 * h + 128],
                        sb_identb[:, :]), inc="t" if h == HG - 1 else None)
                w.wait("vector", "t", t_kntrA)
                v_knfA = V(lambda E, knf_base=knf_base: E.tensor_copy(
                    bfv(knf_base, 1024, 0), pB.bitcast(BF16)[:, 0:1024]), inc="v")
                w.wait("tensor", "v", v_stgk)
                for h in range(HG):
                    t_kntr[(g, j)] = Tn(lambda E, h=h, par=par: E.transpose(
                        pBb[:, 1024 + 128 * h:1024 + 128 * h + 128],
                        ar(OFF_KTOK + 512 * (1 - par), 512).bitcast(BF16)[:, 128 * h:128 * h + 128],
                        sb_identb[:, :]), inc="t" if h == HG - 1 else None)
                w.wait("vector", "t", t_kntr[(g, j)])
                v_knfB = V(lambda E, knf_base=knf_base: E.tensor_copy(
                    bfv(knf_base, 1024, 1024),
                    pB.bitcast(BF16)[:, 1024:2048]), inc="v")

                def emit_pv(qt):
                    qpar = qt % 2
                    w.wait("tensor", "s", exp_done[(g, j, qt)])
                    w.wait("tensor", "v", v_vn)
                    w.wait("tensor", "v", last_evac.get(qpar))
                    for h in range(HG):
                        col = 65 * (h % 4) + 512 * (h // 4)
                        r2 = Tn(lambda E, h=h, col=col, qpar=qpar, par=par: E.matmul(
                            pdst_map[qpar][:, col:col + 65],
                            ar(OFF_P + 1024 * qpar + 128 * h, 128),
                            ar(OFF_VN + 520 * par, 520)[:, 65 * h:65 * h + 65],
                            start=True, stop=True),
                            inc="t" if h == HG - 1 else None)
                    pv_done[(g, j, qt)] = r2
                    last_pv[qpar] = r2
                    w.wait("vector", "t", pv_done[(g, j, qt)])
                    if j == 0:
                        V(lambda E, qt=qt, g=g, qpar=qpar: E.tensor_copy(
                            o_tile(qt, g)[:, 0:260], pdst_map[qpar][:, 0:260]))
                        evac_done[(g, j, qt)] = V(lambda E, qt=qt, g=g, qpar=qpar: E.tensor_copy(
                            o_tile(qt, g)[:, 260:520], pdst_map[qpar][:, 512:772]), inc="v")
                    else:
                        V(lambda E, qt=qt, g=g, qpar=qpar: E.tensor_add(
                            o_tile(qt, g)[:, 0:260], o_tile(qt, g)[:, 0:260],
                            pdst_map[qpar][:, 0:260]))
                        evac_done[(g, j, qt)] = V(lambda E, qt=qt, g=g, qpar=qpar: E.tensor_add(
                            o_tile(qt, g)[:, 260:520], o_tile(qt, g)[:, 260:520],
                            pdst_map[qpar][:, 512:772]), inc="v")
                    last_evac[qpar] = evac_done[(g, j, qt)]

                qt_seq = list(range(tt_n))
                if tt_n >= 4:
                    qt_seq = list(range(tt_n - 2)) + [tt_n - 1, tt_n - 2]
                for pos, qt in (enumerate(qt_seq) if qt_mode >= 1 else ()):
                    qpar = qt % 2
                    w.wait("tensor", "v", v_knfB)
                    if qt == 0 and j == 0:
                        w.wait("tensor", "v", v_qnf)
                    w.wait("tensor", "s", last_exp.get(qpar))
                    for h in range(HG):
                        Tn(lambda E, h=h, qt=qt, par=par, qpar=qpar:
                           E.matmul(pS[qpar][:, 128 * h:128 * h + 128],
                                    bfv(OFF_KNF + 1024 * par, 128, 128 * h),
                                    bfv(OFF_QNF, 128, 2048 * h + 128 * qt),
                                    start=True, stop=False))
                        t_sim = Tn(lambda E, h=h, qt=qt, par=par, qpar=qpar:
                                   E.matmul(pS[qpar][:, 128 * h:128 * h + 128],
                                            bfv(OFF_KNF + 1024 * par, 128, 1024 + 128 * h),
                                            bfv(OFF_QNF, 128, 2048 * h + 128 * qt),
                                            start=False, stop=True),
                                   inc="t" if h == HG - 1 else None)
                    if pos == tt_n - 1:
                        sim_last[(g, j)] = t_sim
                    if qt_mode >= 2 and pos >= 1:
                        emit_pv(qt_seq[pos - 1])
                    w.wait("scalar", "t", t_sim)
                    if qt_mode >= 2:
                        w.wait("scalar", "t", last_pv.get(qpar))
                    exp_done[(g, j, qt)] = Sc(lambda E, qpar=qpar: E.activation(
                        ar(OFF_P + 1024 * qpar, 1024), pS[qpar][:, :], AF.Exp,
                        bias=sb_mb[:, 0:1]), inc="s")
                    last_exp[qpar] = exp_done[(g, j, qt)]
                    if qt_mode < 2:
                        pv_done[(g, j, qt)] = None
                        evac_done[(g, j, qt)] = None
                    if pos == 0 and j + 1 < nb and j >= 1:
                        emit_xdma(j + 1)
                    if pos in (2, 9):
                        presplit_step()
                    if ln_slot is not None and j + 1 < nb and 5 <= pos < 13:
                        if pos == 5:
                            pending_ln[:] = ln_step_list(j + 1)
                        pending_ln.pop(0)()
                if qt_mode >= 2 and tt_n >= 1:
                    emit_pv(qt_seq[-1])
                blk_last_pv[(g, j)] = w.c["t"]
                if ln_slot is None and j + 1 < nb and j >= 1:
                    emit_xdma(j + 1)

        if stage < 4:
            joins = dict(w.c)
            for sem in ("t", "v", "s"):
                w.wait("gpsimd", sem, joins[sem])
            for name in dumps:
                Gp(lambda E, name=name: E.dma_start(
                    out=dump_drams[name][:, :], in_=dump_specs[name][0]), inc="st")
            w.wait("gpsimd", "st", w.c["st"])
            w.wait("sync", "st", w.c["st"])
            return
        joins = dict(w.c)
        for r in ROLES:
            for sem in ("t", "v", "s"):
                w.wait(r, sem, joins[sem])
        w.wait("sync", "g", w.c["g"])
        presplit_flush("wo")
        Sy(lambda E: E.dma_start(
            out=ar(OFF_KVW, 4096).bitcast(BF16).rearrange("p (k n) -> p k n", k=16),
            in_=d_wsp_hi[:, 24:40, :]), inc="w")
        wo_w = w.c["w"]
        w.wait("gpsimd", "pw", pw_ready["wo"])
        Gp(lambda E: E.dma_start(
            out=ar(OFF_KVW + 4096, 4096).bitcast(BF16).rearrange("p (k n) -> p k n", k=16),
            in_=d_wsp_lo[:, 24:40, :]), inc="x")
        wo_x = w.c["x"]
        v_x1 = {}
        v_of = {}
        v_aoread = {}
        t_tr_o = {}
        g_on = {}
        v_rq = {}

        def o_head(tt):
            par = tt % 2
            o_norm = ar(OFF_P + 1024 * par, 1024)
            for g2 in range(NG):
                ot = o_tile(tt, g2).rearrange("p (h c) -> p h c", h=8)
                V(lambda E, ot=ot, g2=g2: E.tensor_copy(
                    ST8["ssq"][:, 8 * g2:8 * g2 + 8], ot[:, :, 64:65].squeeze(2)))
            w.drain("vector")
            if tt >= 1:
                w.wait("vector", "g", g_on[tt - 1])
            for g2 in range(NG):
                v_rq[tt] = V(lambda E, g2=g2: E.reciprocal(
                    ST8["rq"][:, 8 * g2:8 * g2 + 8],
                    ST8["ssq"][:, 8 * g2:8 * g2 + 8]),
                    inc="v" if g2 == NG - 1 else None)
            w.wait("gpsimd", "v", v_rq[tt])
            if tt >= 2:
                w.wait("gpsimd", "t", t_tr_o[tt - 2])
            for g2 in range(NG):
                ot = o_tile(tt, g2).rearrange("p (h c) -> p h c", h=8)
                g_on[tt] = w.op("gpsimd", lambda E, ot=ot, g2=g2, o_norm=o_norm:
                                E.tensor_mul(
                    o_norm[:, 512 * g2:512 * g2 + 512].rearrange("p (h c) -> p h c", h=8),
                    ot[:, :, 0:64],
                    ST8["rq"][:, 8 * g2:8 * g2 + 8].unsqueeze(2).to_broadcast([128, 8, 64])),
                    inc="g" if g2 == NG - 1 else None)
            v_of[tt], t_tr_o[tt] = transpose8(
                lambda c, par=par: ar(OFF_P + 1024 * par + 128 * c, 128),
                ar(OFF_KNF + 1024 * par, 1024), 8, t_deps=[("g", g_on[tt])],
                v_extra=([("t", t_opmm[tt - 2])] if tt >= 2 else ()),
                split_to=(bfv(OFF_KNF + 1024 * par, 1024),
                          bfv(OFF_KNF + 1024 * par, 1024, 1024)))
            if tt >= 2:
                w.wait("gpsimd", "v", v_x1[tt - 2])
            Gp(lambda E, tt=tt, par=par: E.dma_start(
                out=ar(OFF_XBLK + 1024 * par, 1024),
                in_=dx_q[128 * tt:128 * tt + 128, :]), inc="x")
            x_cnt[("o", tt)] = w.c["x"]

        t_opmm = {}

        def o_tail(tt):
            par = tt % 2
            ao = ar(OFF_XN + 1024 * par, 1024)
            w.wait("tensor", "v", v_of[tt])
            w.wait("tensor", "w", wo_w)
            w.wait("tensor", "x", wo_x)
            w.wait("tensor", "v", v_aoread.get(tt - 1))
            oh = lambda c, par=par: bfv(OFF_KNF + 1024 * par, 128, 128 * c)
            ol = lambda c, par=par: bfv(OFF_KNF + 1024 * par, 128, 1024 + 128 * c)
            for nh in range(2):
                t_op = mm3(pA[:, 512 * nh:512 * nh + 512], oh, ol,
                           lambda c, nh=nh: bfv(OFF_KVW, 512, 1024 * c + 512 * nh),
                           lambda c, nh=nh: bfv(OFF_KVW, 512, 8192 + 1024 * c + 512 * nh),
                           NCH, inc_last=(nh == 1))
            t_opmm[tt] = t_op
            w.wait("vector", "t", t_op)
            v_aoread[tt] = V(lambda E, ao=ao: E.tensor_add(ao, pA[:, :], sb_bo[:, :]),
                             inc="v")
            ln_chain(par, ao, ao, "mu", "sd", "rs", False,
                     s_dep=[("v", v_aoread[tt])])
            w.wait("vector", "x", x_cnt[("o", tt)])
            w.drain("vector")
            v_x1[tt] = V(lambda E, tt=tt, par=par, ao=ao: E.tensor_add(
                x1_tile(tt), ao, ar(OFF_XBLK + 1024 * par, 1024)), inc="v")

        for tt in range(tt_n):
            o_head(tt)
            if tt >= 1:
                o_tail(tt - 1)
        o_tail(tt_n - 1)

        if stage < 5:
            joins = dict(w.c)
            for sem in ("t", "v", "s"):
                w.wait("gpsimd", sem, joins[sem])
            for name in dumps:
                Gp(lambda E, name=name: E.dma_start(
                    out=dump_drams[name][:, :], in_=dump_specs[name][0]), inc="st")
            w.wait("gpsimd", "st", w.c["st"])
            w.wait("sync", "st", w.c["st"])
            return
        joins = dict(w.c)
        for r in ROLES:
            for sem in ("t", "v", "s"):
                w.wait(r, sem, joins[sem])
        l3_off = {0: OFF_VN, 1: OFF_KTOK}
        bank_off = [
            [OFF_XBLK + 512 * c for c in range(8)],
            [OFF_XNF + 512 * c for c in range(4)] + [OFF_P + 512 * c for c in range(4)],
        ]
        n_tg = max(1, tt_n // 8)
        exp_steps = [(nh, e) for _tg in range(n_tg) for nh in range(2)
                     for e in range(E)]
        step_t = {}
        w_after = {}

        def load_emit(s):
            nh, e = exp_steps[s]
            bk = s % 2
            if s >= 2:
                w.wait("sync", "t", step_t[s - 2])
            if bk == 0:
                Sy(lambda E_, e=e, nh=nh: E_.dma_start(
                    out=ar(OFF_XBLK, 4096).bitcast(BF16),
                    in_=d_ews[e, nh, :, :]), inc="w")
            else:
                Sy(lambda E_, e=e, nh=nh: E_.dma_start(
                    out=ar(OFF_XNF, 2048).bitcast(BF16),
                    in_=d_ews[e, nh, :, 0:4096]), inc="w")
                Sy(lambda E_, e=e, nh=nh: E_.dma_start(
                    out=ar(OFF_P, 2048).bitcast(BF16),
                    in_=d_ews[e, nh, :, 4096:8192]), inc="w")
            w_after[s] = w.c["w"]

        store_cnt = {}
        for tg in range(max(1, tt_n // 8)):
            tiles = list(range(8 * tg, min(8 * tg + 8, tt_n)))
            if tg > 0:
                w.wait("vector", "t", w.c["t"])
                w.wait("vector", "s", w.c["s"])
            v_l3d = {}
            t_l3tr = {}

            def head_steps(i, tt):
                par = tt % 2
                v_pre = [("t", t_l3tr[i - 2])] if i >= 2 else None
                steps, st = ln_chain_steps(par, x1_tile(tt), ar(l3_off[par], 1024),
                                           "mu", "sd", "rs", False, v_pre=v_pre)

                def fin():
                    w.drain("vector")
                    v_l3d[i] = V(lambda E, i=i, par=par: E.tensor_copy(
                        ar(OFF_KVW + 1024 * i, 1024), ar(l3_off[par], 1024)),
                        inc="v")
                return steps, fin

            def ffn_tail(i, tt):
                par = tt % 2
                v_f, t_l3tr[i] = transpose8(
                    lambda c, par=par: ar(l3_off[par] + 128 * c, 128),
                    ar(OFF_QNF + 1024 * i, 1024), 8, t_deps=[("v", v_l3d[i])],
                    split_to=(bfv(OFF_QNF + 1024 * i, 1024),
                              bfv(OFF_QNF + 1024 * i, 1024, 1024)))
                w.wait("tensor", "v", v_f)
                w.wait("tensor", "g", g_gw)
                t_g = mm3(pS0[:, 0:8],
                          lambda c, i=i: bfv(OFF_QNF + 1024 * i, 128, 128 * c),
                          lambda c, i=i: bfv(OFF_QNF + 1024 * i, 128, 1024 + 128 * c),
                          lambda c: sb_gwb[:, 8 * c:8 * c + 8],
                          lambda c: sb_gwb[:, 64 + 8 * c:64 + 8 * c + 8], NCH)
                w.wait("vector", "t", t_g)
                V(lambda E, i=i: E.tensor_copy(sb_sc[:, 8 * i:8 * i + 8], pS0[:, 0:8]),
                  inc="v")

            hsts = {0: head_steps(0, tiles[0])}
            hsts[0][0][0]()
            for i, tt in enumerate(tiles):
                steps, fin = hsts[i]
                steps[1]()
                if i >= 1:
                    ffn_tail(i - 1, tiles[i - 1])
                if i + 1 < len(tiles):
                    hsts[i + 1] = head_steps(i + 1, tiles[i + 1])
                    hsts[i + 1][0][0]()
                steps[2]()
                steps[3]()
                fin()
            ffn_tail(len(tiles) - 1, tiles[-1])
            w.drain("vector")
            for i in range(len(tiles)):
                V(lambda E, i=i: E.tensor_reduce(sb_mx[:, i:i + 1], sb_sc[:, 8 * i:8 * i + 8],
                                                 AX.X, ALU.max))
            w.drain("vector")
            for i in range(len(tiles)):
                v_masks = V(lambda E, i=i: E.tensor_scalar(sb_masks[:, 8 * i:8 * i + 8],
                                                           sb_sc[:, 8 * i:8 * i + 8],
                                                           sb_mx[:, i:i + 1], None,
                                                           ALU.is_equal),
                            inc="v" if i == len(tiles) - 1 else None)
            if tg == 0:
                presplit_flush("ew")
                load_emit(0)
                load_emit(1)
            v_acc = None
            v_acc_par = {}
            for local_s in range(16):
                s = 16 * tg + local_s
                nh, e = exp_steps[s]
                bk = s % 2
                w.wait("tensor", "w", w_after[s])
                w.wait("tensor", "v", v_masks)
                for i in range(len(tiles)):
                    pd = pdst_map[i % 2]
                    w.wait("tensor", "v", v_acc_par.get(i % 2))
                    t_mm = mm3(pd[:, 0:512],
                               lambda c, i=i: bfv(OFF_QNF + 1024 * i, 128, 128 * c),
                               lambda c, i=i: bfv(OFF_QNF + 1024 * i, 128,
                                                  1024 + 128 * c),
                               lambda c, bk=bk: bfv(bank_off[bk][c], 512, 0),
                               lambda c, bk=bk: bfv(bank_off[bk][c], 512, 512),
                               NCH)
                    w.wait("vector", "t", t_mm)
                    v_acc = V(lambda E, i=i, e=e, nh=nh, pd=pd: E.scalar_tensor_tensor(
                        ar(OFF_KVW + 1024 * i + 512 * nh, 512), pd[:, 0:512],
                        sb_masks[:, 8 * i + e:8 * i + e + 1],
                        ar(OFF_KVW + 1024 * i + 512 * nh, 512),
                        ALU.mult, ALU.add), inc="v")
                    v_acc_par[i % 2] = v_acc
                step_t[s] = w.c["t"]
                if s + 2 < 16 * n_tg:
                    load_emit(s + 2)
            w.wait("vector", "v", v_acc)

            def y_steps(i, tt):
                par = tt % 2
                v_pre = ([("st", store_cnt.get(par, 0))]
                         if (tg, i) not in ((0, 0), (0, 1)) else None)
                return ln_chain_steps(par, ar(OFF_KVW + 1024 * i, 1024),
                                      ar(OFF_KNF + 1024 * par, 1024),
                                      "mu", "sd", "rs", False,
                                      s_dep=[("v", w.c["v"])], v_pre=v_pre)

            ysts = {0: y_steps(0, tiles[0])}
            ysts[0][0][0]()
            for i, tt in enumerate(tiles):
                par = tt % 2
                yb = ar(OFF_KNF + 1024 * par, 1024)
                steps, _st = ysts[i]
                steps[1]()
                if i + 1 < len(tiles):
                    ysts[i + 1] = y_steps(i + 1, tiles[i + 1])
                    ysts[i + 1][0][0]()
                steps[2]()
                steps[3]()
                w.drain("vector")
                v_y = V(lambda E, yb=yb, tt=tt: E.tensor_add(yb, yb, x1_tile(tt)), inc="v")
                w.wait("sync", "v", v_y)
                Sy(lambda E, yb=yb, tt=tt: E.dma_start(out=d_y[128 * tt:128 * tt + 128, :],
                                                       in_=yb), inc="st")
                store_cnt[par] = w.c["st"]

        joins = dict(w.c)
        for sem in ("t", "v", "s"):
            w.wait("gpsimd", sem, joins[sem])
        for name in dumps:
            Gp(lambda E, name=name: E.dma_start(
                out=dump_drams[name][:, :], in_=dump_specs[name][0]), inc="sd")
        w.wait("gpsimd", "sd", w.c["sd"])
        w.wait("gpsimd", "st", w.c["st"])
        w.wait("sync", "st", w.c["st"])

    with nc.Block() as block:
        block.sync(lambda E: schedule(Walk(nc, sems, "sync", E)))
        block.gpsimd(lambda E: schedule(Walk(nc, sems, "gpsimd", E)))
        block.tensor(lambda E: schedule(Walk(nc, sems, "tensor", E)))
        block.vector(lambda E: schedule(Walk(nc, sems, "vector", E)))
        block.scalar(lambda E: schedule(Walk(nc, sems, "scalar", E)))
    return nc


_prog_cache = {}


def _get_prog(dumps=(), nb=NB, tt_n=TT, stage=5, qt_mode=2):
    key = (tuple(dumps), nb, tt_n, stage, qt_mode)
    if key not in _prog_cache:
        _prog_cache[key] = build_program(tuple(dumps), nb, tt_n, stage, qt_mode)
    return _prog_cache[key]


def kernel(L0_attn_x, L0_attn_ln_gamma, L0_attn_ln_beta, L0_attn_q_gamma, L0_attn_k_gamma,
           L0_attn_w_q, L0_attn_w_k, L0_attn_w_v, L0_attn_w_o, L0_attn_b_o,
           L0_ffn_gate_w, L0_ffn_expert_w, _dumps=(), _results_out=None, _nb=NB, _tt_n=TT, _stage=5, _qt_mode=2):
    x = np.asarray(L0_attn_x, dtype=np.float32)
    wq = np.ascontiguousarray(np.asarray(L0_attn_w_q, np.float32).T)
    wk = np.ascontiguousarray(np.asarray(L0_attn_w_k, np.float32).T)
    wv = np.ascontiguousarray(np.asarray(L0_attn_w_v, np.float32).T)
    wo = np.ascontiguousarray(np.asarray(L0_attn_w_o, np.float32).T)
    gw = np.ascontiguousarray(np.asarray(L0_ffn_gate_w, np.float32).T)
    ew = np.ascontiguousarray(np.transpose(np.asarray(L0_ffn_expert_w, np.float32), (0, 2, 1)))
    lng = np.ascontiguousarray(np.asarray(L0_attn_ln_gamma, np.float32))
    lnb = np.ascontiguousarray(np.asarray(L0_attn_ln_beta, np.float32))
    bo = np.ascontiguousarray(np.asarray(L0_attn_b_o, np.float32))
    qg8 = np.ascontiguousarray(np.tile(np.asarray(L0_attn_q_gamma, np.float32), 8))
    kg8 = np.ascontiguousarray(np.tile(np.asarray(L0_attn_k_gamma, np.float32), 8))

    nc = _get_prog(tuple(_dumps), _nb, _tt_n, _stage, _qt_mode)
    in_maps = []
    for c in range(8):
        b, sl = c // 4, c % 4
        in_maps.append(dict(
            xq=np.ascontiguousarray(x[b, T * sl:T * (sl + 1)]),
            xkv=np.ascontiguousarray(x[b]),
            wq=wq, wk=wk, wv=wv, wo=wo, lng=lng, lnb=lnb, bo=bo,
            qg8=qg8, kg8=kg8, gw=gw, ew=ew,
        ))
    res = run_bass_kernel_spmd(nc, in_maps, core_ids=list(range(8)))
    if _results_out is not None:
        _results_out.append(res)
    y = np.empty((B, S, D), dtype=np.float32)
    for c in range(8):
        b, sl = c // 4, c % 4
        y[b, T * sl:T * (sl + 1)] = res.results[c]["y"]
    return y
